# version 02
# baseline (speedup 1.0000x reference)
import numpy as np
import contextlib
import concourse.bass as bass
import concourse.mybir as mybir
from concourse.bass_utils import run_bass_kernel_spmd

F32 = mybir.dt.float32
BF16 = mybir.dt.bfloat16
I8 = mybir.dt.int8
I32 = mybir.dt.int32
FP8 = mybir.dt.float8e4
AF = mybir.ActivationFunctionType
ALU = mybir.AluOpType
AX = mybir.AxisListType

T = 4096
DM = 1024
DI = 2048
DS = 16
HF = 4096
CH = 512
NCHUNK = T // CH

KEYS = ("t", "v", "s", "g", "yd", "gd")
ENG_OF = {"t": "tensor", "v": "vector", "s": "scalar", "g": "gpsimd",
          "yd": "sync", "gd": "gpsimd"}
DMA_KEYS = ("yd", "gd")


class Plan:
    def __init__(self, nc):
        self.nc = nc
        self.recs = []

    def op(self, key, fn, reads=(), writes=()):
        self.recs.append((key, fn, tuple(reads), tuple(writes)))

    def barrier(self):
        self.recs.append(("BARRIER", None, (), ()))

    def finalize(self, block, sems, esems):
        recs = self.recs
        cnt = {k: 0 for k in KEYS}
        bufs = {}
        waited = {k: {} for k in KEYS}
        needed = set()
        rec_waits = []
        for rec in recs:
            key, fn, reads, writes = rec
            if key == "BARRIER":
                bw = {k: cnt[k] for k in KEYS if cnt[k] > 0}
                rec_waits.append(bw)
                for k in KEYS:
                    for pk, ps in bw.items():
                        if pk != k and waited[k].get(pk, 0) < ps:
                            waited[k][pk] = ps
                            needed.add((pk, ps))
                continue
            seq = cnt[key] + 1
            need = {}
            drain = False

            def adddep(pk, ps):
                nonlocal drain
                if pk != key or pk in DMA_KEYS:
                    if pk != key or ps < seq:
                        if need.get(pk, 0) < ps:
                            need[pk] = ps
                elif ps == seq - 1:
                    drain = True

            for b in reads:
                e = bufs.get(b)
                if e and e["w"] is not None:
                    adddep(*e["w"])
            for b in writes:
                e = bufs.get(b)
                if e:
                    if e["w"] is not None:
                        adddep(*e["w"])
                    for (rk, rs) in e["r"]:
                        adddep(rk, rs)
            fw = []
            for pk, ps in need.items():
                if waited[key].get(pk, 0) < ps:
                    fw.append((pk, ps))
                    waited[key][pk] = ps
                    needed.add((pk, ps))
            rec_waits.append((fw, drain))
            cnt[key] = seq
            for b in reads:
                bufs.setdefault(b, {"w": None, "r": []})["r"].append((key, seq))
            for b in writes:
                bufs[b] = {"w": (key, seq), "r": []}
        do_inc = {}
        c2 = {k: 0 for k in KEYS}
        for rec in recs:
            k = rec[0]
            if k == "BARRIER":
                continue
            c2[k] += 1
            do_inc[(k, c2[k])] = (k in DMA_KEYS) or ((k, c2[k]) in needed)
        incs_upto = {}
        for k in KEYS:
            arr = [0] * (cnt[k] + 1)
            acc = 0
            for s in range(1, cnt[k] + 1):
                if do_inc[(k, s)]:
                    acc += 16 if k in DMA_KEYS else 1
                arr[s] = acc
            incs_upto[k] = arr
        echo_ps = {k: set() for k in DMA_KEYS}
        for rw in rec_waits:
            if isinstance(rw, dict):
                continue
            fw, _dr = rw
            for pk, ps in fw:
                if pk in DMA_KEYS:
                    echo_ps[pk].add(ps)
        echo_rank = {}
        for k in DMA_KEYS:
            for r_i, ps in enumerate(sorted(echo_ps[k])):
                echo_rank[(k, ps)] = r_i + 1
        eng_lists = {"tensor": [], "vector": [], "scalar": [], "gpsimd": [], "sync": []}
        c3 = {k: 0 for k in KEYS}
        for i, rec in enumerate(recs):
            key = rec[0]
            if key == "BARRIER":
                for en in eng_lists:
                    eng_lists[en].append(("BWAIT", rec_waits[i], en))
                continue
            c3[key] += 1
            eng_lists[ENG_OF[key]].append((key, c3[key], rec[1], rec_waits[i]))

        def emit(en, eng):
            last_b = None
            for item in eng_lists[en]:
                if item[0] == "BWAIT":
                    bw = item[1]
                    if id(bw) == last_b:
                        continue
                    last_b = id(bw)
                    for pk, ps in bw.items():
                        if ENG_OF[pk] == en and pk not in DMA_KEYS:
                            continue
                        vv = incs_upto[pk][ps]
                        if vv > 0:
                            eng.wait_ge(sems[pk], vv)
                    continue
                key, seq, fn, fwd = item
                fw, drain = fwd
                if drain and key in ("v", "g"):
                    eng.drain()
                for pk, ps in fw:
                    if pk in DMA_KEYS:
                        eng.wait_ge(esems[pk], echo_rank[(pk, ps)])
                        continue
                    if ENG_OF[pk] == en:
                        continue
                    vv = incs_upto[pk][ps]
                    if vv > 0:
                        eng.wait_ge(sems[pk], vv)
                instr = fn(eng)
                if do_inc[(key, seq)]:
                    instr.then_inc(sems[key], 16 if key in DMA_KEYS else 1)
                if key in DMA_KEYS and seq in echo_ps[key]:
                    eng.wait_ge(sems[key], incs_upto[key][seq])
                    eng.sem_inc(esems[key], 1)

        @block.sync
        def _(e):
            emit("sync", e)

        @block.tensor
        def _(e):
            emit("tensor", e)

        @block.vector
        def _(e):
            emit("vector", e)

        @block.scalar
        def _(e):
            emit("scalar", e)

        @block.gpsimd
        def _(e):
            emit("gpsimd", e)


def build_nc():
    nc = bass.Bass("TRN2", target_bir_lowering=False)
    x_in = nc.dram_tensor("x", [T, DM], F32, kind="ExternalInput").ap()
    out = nc.dram_tensor("out", [T, DM], F32, kind="ExternalOutput").ap()
    W = {}
    fspecs = [
        ("L0_attn_Wq", [DM, 128]), ("L0_attn_bq", [128]),
        ("L0_attn_Wk", [DM, 128]), ("L0_attn_bk", [128]),
        ("L0_attn_Wv", [DM, 128]), ("L0_attn_bv", [128]),
        ("L0_attn_Wa", [DM, 1]), ("L0_attn_ba", [1]),
        ("L0_attn_Wb", [DM, 1]), ("L0_attn_bb", [1]),
        ("L0_attn_Wg", [DM, 128]), ("L0_attn_bg", [128]),
        ("L0_attn_Wo", [128, DM]), ("L0_attn_bo", [DM]),
        ("L0_ffn_ln_gamma", [DM]), ("L0_ffn_ln_beta", [DM]),
        ("L0_ffn_fc1_scales", [8, HF]), ("L0_ffn_fc1_bias", [HF]),
        ("L0_ffn_fc2_scales", [32, DM]), ("L0_ffn_fc2_bias", [DM]),
        ("L1_attn_ln_g", [DM]), ("L1_attn_ln_b", [DM]),
        ("L1_attn_W_in", [DM, 2 * DI]),
        ("L1_attn_conv_w", [DI, 4]), ("L1_attn_conv_b", [DI]),
        ("L1_attn_W_xp", [DI, 33]),
        ("L1_attn_W_dt", [1, DI]), ("L1_attn_b_dt", [DI]),
        ("L1_attn_A_log", [DI, DS]), ("L1_attn_D", [DI]),
        ("L1_attn_W_out", [DI, DM]),
        ("L1_ffn_w_gate", [HF, DM]), ("L1_ffn_w_up", [HF, DM]),
        ("L1_ffn_w_down", [DM, HF]),
    ]
    for nm, shp in fspecs:
        W[nm] = nc.dram_tensor(nm, shp, F32, kind="ExternalInput").ap()
    W["L0_ffn_fc1_q"] = nc.dram_tensor("L0_ffn_fc1_q", [DM, HF], I32,
                                       kind="ExternalInput").ap()
    W["L0_ffn_fc2_q"] = nc.dram_tensor("L0_ffn_fc2_q", [HF, DM], I32,
                                       kind="ExternalInput").ap()
    xA = nc.dram_tensor("xA", [T, DM], F32, kind="Internal").ap()
    xB = nc.dram_tensor("xB", [T, DM], F32, kind="Internal").ap()
    xC = nc.dram_tensor("xC", [T, DM], F32, kind="Internal").ap()
    w1d = nc.dram_tensor("w1d", [DM, HF], BF16, kind="Internal").ap()
    w2d = nc.dram_tensor("w2d", [HF, DM], BF16, kind="Internal").ap()
    qgTd = nc.dram_tensor("qgTd", [DM, HF], FP8, kind="Internal").ap()
    quTd = nc.dram_tensor("quTd", [DM, HF], FP8, kind="Internal").ap()
    qdTd = nc.dram_tensor("qdTd", [HF, DM], FP8, kind="Internal").ap()
    zsd = nc.dram_tensor("zsd", [DI, T], BF16, kind="Internal").ap()
    xpjd = nc.dram_tensor("xpjd", [33, 512], F32, kind="Internal").ap()

    es = contextlib.ExitStack()
    with es:
        def sb(name, shape, dt):
            return es.enter_context(nc.sbuf_tensor(name, shape, dt))
        ident_bf = sb("ident_bf", [128, 128], BF16)
        inclT_f = sb("inclT_f", [128, 128], F32)
        maskUneg = sb("maskUneg", [128, 128], F32)
        maskLneg = sb("maskLneg", [128, 128], F32)
        iot = sb("iot", [128, 128], F32)
        ones_row_f = sb("ones_row_f", [1, 128], F32)
        ones_col_f = sb("ones_col_f", [128, 1], F32)
        wqkvg = [sb(f"wqkvg{k}", [128, 512], BF16) for k in range(8)]
        wab = [sb(f"wab{k}", [128, 2], BF16) for k in range(8)]
        wo_b = sb("wo_b", [128, DM], BF16)
        bo_bc = sb("bo_bc", [128, DM], F32)
        bqkvg_bc = sb("bqkvg_bc", [128, 512], F32)
        bab_bc = sb("bab_bc", [128, 2], F32)
        g0_bc = sb("g0_bc", [128, DM], BF16)
        b0_bc = sb("b0_bc", [128, DM], BF16)
        g1_bc = sb("g1_bc", [128, DM], BF16)
        b1_bc = sb("b1_bc", [128, DM], BF16)
        fc1b_fm = sb("fc1b_fm", [128, 32], F32)
        fc2b_bc4 = sb("fc2b_bc4", [128, DM], F32)
        convw = sb("convw", [128, 64], F32)
        convb = sb("convb", [128, 16], F32)
        wdt_c = sb("wdt_c", [128, 16], F32)
        bdt_c = sb("bdt_c", [128, 16], F32)
        A_c = sb("A_c", [128, 16 * DS], F32)
        D_c = sb("D_c", [128, 16], F32)
        wxp = [sb(f"wxp{k}", [128, 33], BF16) for k in range(16)]
        hstate = sb("hstate", [128, 16 * DS], F32)
        halo = sb("halo", [128, 16 * 3], BF16)
        Sst = sb("Sst", [128, 128], F32)
        STst = sb("STst", [128, 128], F32)
        S_bf = sb("S_bf", [128, 128], BF16)
        ST_bf = sb("ST_bf", [128, 128], BF16)
        mug_c = sb("mug_c", [128, 1], F32)
        muu_c = sb("muu_c", [128, 1], F32)
        mud_h = sb("mud_h", [128, 1], F32)
        swg_c = sb("swg_c", [128, 1], F32)
        swu_c = sb("swu_c", [128, 1], F32)
        swd_c = sb("swd_c", [128, 1], F32)
        colw = sb("colw", [128, 96], F32)
        epsc = sb("epsc", [128, 1], F32)
        nhalfc = sb("nhalfc", [128, 1], F32)
        onec = sb("onec", [128, 1], F32)
        rowst = sb("rowst", [1, 512], F32)

        psum = []
        for i in range(8):
            if i in (2, 7):
                psum.append(es.enter_context(
                    nc.psum_tensor(f"pb{i}", [128, 1024], BF16)))
            else:
                psum.append(es.enter_context(
                    nc.psum_tensor(f"pb{i}", [128, 512], F32)))
        BK = [f"bank{i}" for i in range(8)]
        sems = {k: es.enter_context(nc.semaphore(f"sem_{k}")) for k in KEYS}
        esems = {k: es.enter_context(nc.semaphore(f"esem_{k}")) for k in DMA_KEYS}
        block = es.enter_context(nc.Block())
        P = Plan(nc)

        def ts(outap, inap, s1, s2, op0, op1, rd, wr):
            if op1 is None:
                P.op("v", lambda e, _o=outap, _i=inap, _a=s1:
                     e.tensor_scalar(_o, _i, _a, None, op0), rd, wr)
            else:
                P.op("v", lambda e, _o=outap, _i=inap, _a=s1, _b=s2:
                     e.tensor_scalar(_o, _i, _a, _b, op0, op1), rd, wr)

        def tt(outap, a, b, opx, rd, wr):
            P.op("v", lambda e, _o=outap, _a=a, _b=b:
                 e.tensor_tensor(_o, _a, _b, opx), rd, wr)

        def stt(outap, in0, sc, in1, op0, op1, rd, wr):
            P.op("v", lambda e, _o=outap, _i=in0, _s=sc, _j=in1:
                 e.scalar_tensor_tensor(_o, _i, _s, _j, op0, op1), rd, wr)

        def act(outap, inap, func, rd, wr, bias=None, scale=None, accum=None):
            kw = {}
            if bias is not None:
                kw["bias"] = bias
            if scale is not None:
                kw["scale"] = scale
            if accum is not None:
                kw["accum_out"] = accum
            P.op("s", lambda e, _o=outap, _i=inap, _k=kw:
                 e.activation(_o, _i, func, **_k), rd, wr)

        def vcopy(outap, inap, rd, wr):
            P.op("v", lambda e, _o=outap, _i=inap: e.tensor_copy(_o, _i), rd, wr)

        def scopy(outap, inap, rd, wr):
            P.op("s", lambda e, _o=outap, _i=inap: e.copy(_o, _i), rd, wr)

        def vrecip(outap, inap, rd, wr):
            P.op("v", lambda e, _o=outap, _i=inap: e.reciprocal(_o, _i), rd, wr)

        def mm(outp, lhsT, rhs, start, stop, rd, wr):
            P.op("t", lambda e, _o=outp, _l=lhsT, _r=rhs, _s=start, _p=stop:
                 e.matmul(_o, _l, _r, start=_s, stop=_p), rd, wr)

        def tr(outp, inap, rd, wr):
            P.op("t", lambda e, _o=outp, _i=inap:
                 e.transpose(_o, _i, ident_bf[:]), rd, wr)

        def dma(key, outap, inap, rd, wr):
            P.op(key, lambda e, _o=outap, _i=inap: e.dma_start(_o, _i), rd, wr)

        def vred(outap, inap, op, absv, rd, wr):
            P.op("v", lambda e, _o=outap, _i=inap:
                 e.tensor_reduce(_o, _i, AX.X, op, apply_absolute_value=absv),
                 rd, wr)

        def memset(ap_, val, wr):
            P.op("v", lambda e, _o=ap_, _v=val: e.memset(_o, _v), (), wr)

        def chunk_ap(dram, c):
            return dram[c * CH:(c + 1) * CH, :].rearrange("(a p) d -> p a d", p=128)

        def tile3(tile):
            return tile[:].rearrange("p (a d) -> p a d", a=4)

        def bcast_row(dstap, src_row, n, bank, rdn, wrn, dst_rd_extra=()):
            mm(psum[bank][:, 0:n], ones_row_f[:], src_row, True, True,
               ("ones_row_f",) + tuple(rdn), (BK[bank],))
            vcopy(dstap, psum[bank][:, 0:n], (BK[bank],), wrn)

        P.op("g", lambda e: e.iota(iot[:], [[-1, 128]], base=0,
                                   channel_multiplier=1,
                                   allow_small_or_imprecise_dtypes=True),
             (), ("iot",))
        with contextlib.ExitStack() as stk:
            def psb(name, shape, dt):
                return stk.enter_context(nc.sbuf_tensor(name, shape, dt))
            tmpf = psb("p_tmpf", [128, 128], F32)
            vtmp = psb("p_vtmp", [32, 128], F32)
            vtmpb = psb("p_vtmpb", [32, 128], BF16)

            def load_vec_fm(dst, src1d, nt, wrn):
                dma("yd", vtmp[0:nt, :],
                    src1d.rearrange("(i f) -> i f", f=128), (), ("p_vtmp",))
                vcopy(vtmpb[0:nt, :], vtmp[0:nt, :], ("p_vtmp",), ("p_vtmpb",))
                P.op("t", lambda e, _o=psum[7][:, 0:128], _i=vtmpb[0:nt, :],
                     _id=ident_bf[0:nt, :]: e.transpose(_o, _i, _id),
                     ("p_vtmpb", "ident"), (BK[7],))
                vcopy(dst, psum[7][:, 0:nt], (BK[7],), wrn)
            ts(inclT_f[:], iot[:], 0.0, None, ALU.is_le, None, ("iot",), ("inclT",))
            ts(tmpf[:], iot[:], 0.0, None, ALU.is_lt, None, ("iot",), ("ptmp",))
            ts(maskUneg[:], tmpf[:], -1.0, None, ALU.mult, None, ("ptmp",), ("maskUneg",))
            ts(tmpf[:], iot[:], 0.0, None, ALU.is_gt, None, ("iot",), ("ptmp",))
            ts(maskLneg[:], tmpf[:], -1.0, None, ALU.mult, None, ("ptmp",), ("maskLneg",))
            ts(tmpf[:], iot[:], 0.0, None, ALU.is_equal, None, ("iot",), ("ptmp",))
            vcopy(ident_bf[:], tmpf[:], ("ptmp",), ("ident",))
            memset(ones_row_f[:], 1.0, ("ones_row_f",))
            memset(epsc[:], 1e-5, ("epsc",))
            memset(nhalfc[:], -0.5, ("nhalfc",))
            memset(onec[:], 1.0, ("onec",))
            memset(ones_col_f[:], 1.0, ("ones_col_f",))
            memset(hstate[:], 0.0, ("hstate",))
            memset(halo[:], 0.0, tuple(f"halo{d}" for d in range(16)))
            memset(Sst[:], 0.0, ("Sst",))
            memset(STst[:], 0.0, ("STst",))
            memset(S_bf[:], 0.0, ("S_bf",))
            memset(ST_bf[:], 0.0, ("ST_bf",))
            for k in range(8):
                for j, nm in enumerate(["L0_attn_Wq", "L0_attn_Wk",
                                        "L0_attn_Wv", "L0_attn_Wg"]):
                    dma("gd", wqkvg[k][:, j * 128:(j + 1) * 128],
                        W[nm][k * 128:(k + 1) * 128, :], (), (f"wqkvg{k}",))
                dma("gd", wab[k][:, 0:1], W["L0_attn_Wa"][k * 128:(k + 1) * 128, :],
                    (), (f"wab{k}",))
                dma("gd", wab[k][:, 1:2], W["L0_attn_Wb"][k * 128:(k + 1) * 128, :],
                    (), (f"wab{k}",))
            dma("gd", wo_b[:], W["L0_attn_Wo"], (), ("wo_b",))
            def bc_from(nm, dst, wrn, scale=None, cast=None, width=DM):
                for h in range((width + 511) // 512):
                    n = min(512, width - h * 512)
                    dma("yd", rowst[0:1, 0:n], W[nm][None, h * 512:h * 512 + n],
                        (), ("rowst",))
                    mm(psum[6][:, 0:n], ones_row_f[:], rowst[0:1, 0:n], True, True,
                       ("ones_row_f", "rowst"), (BK[6],))
                    if scale is None:
                        vcopy(dst[:, h * 512:h * 512 + n], psum[6][:, 0:n],
                              (BK[6],), wrn)
                    else:
                        ts(dst[:, h * 512:h * 512 + n], psum[6][:, 0:n], scale, None,
                           ALU.mult, None, (BK[6],), wrn)
            bc_from("L0_attn_bo", bo_bc, ("bo",))
            bc_from("L0_ffn_ln_gamma", g0_bc, ("g0",))
            bc_from("L0_ffn_ln_beta", b0_bc, ("b0",))
            bc_from("L1_attn_ln_g", g1_bc, ("g1",))
            bc_from("L1_attn_ln_b", b1_bc, ("b1",))
            bc_from("L0_ffn_fc2_bias", fc2b_bc4, ("fc2b",), scale=0.25)
            for j, nm in enumerate(["L0_attn_bq", "L0_attn_bk", "L0_attn_bv",
                                    "L0_attn_bg"]):
                dma("yd", rowst[0:1, j * 128:(j + 1) * 128], W[nm][None, :],
                    (), ("rowst",))
            mm(psum[6][:, 0:512], ones_row_f[:], rowst[0:1, 0:512], True, True,
               ("ones_row_f", "rowst"), (BK[6],))
            vcopy(bqkvg_bc[:], psum[6][:, 0:512], (BK[6],), ("bqkvg",))
            dma("yd", rowst[0:1, 0:1], W["L0_attn_ba"][None, :], (), ("rowst",))
            dma("yd", rowst[0:1, 1:2], W["L0_attn_bb"][None, :], (), ("rowst",))
            mm(psum[6][:, 0:2], ones_row_f[:], rowst[0:1, 0:2], True, True,
               ("ones_row_f", "rowst"), (BK[6],))
            vcopy(bab_bc[:], psum[6][:, 0:2], (BK[6],), ("bab",))
            load_vec_fm(fc1b_fm[:], W["L0_ffn_fc1_bias"], 32, ("fc1b",))
            dma("yd", convw[:].rearrange("p (i f) -> p i f", f=4),
                W["L1_attn_conv_w"].rearrange("(i p) f -> p i f", p=128),
                (), ("convw",))
            load_vec_fm(convb[:], W["L1_attn_conv_b"], 16, ("convb",))
            load_vec_fm(wdt_c[:], W["L1_attn_W_dt"][0, :], 16, ("wdt",))
            load_vec_fm(bdt_c[:], W["L1_attn_b_dt"], 16, ("bdt",))
            load_vec_fm(D_c[:], W["L1_attn_D"], 16, ("Dc",))
            dma("yd", A_c[:].rearrange("p (i n) -> p i n", n=DS),
                W["L1_attn_A_log"].rearrange("(i p) n -> p i n", p=128), (), ("Ac",))
            act(A_c[:], A_c[:], AF.Exp, ("Ac",), ("Ac",))
            ts(A_c[:], A_c[:], -1.0, None, ALU.mult, None, ("Ac",), ("Ac",))
            for k in range(16):
                dma("gd", wxp[k][:], W["L1_attn_W_xp"][k * 128:(k + 1) * 128, :],
                    (), (f"wxp{k}",))
        P.barrier()

        with contextlib.ExitStack() as stk:
            def psb(name, shape, dt):
                return stk.enter_context(nc.sbuf_tensor(name, shape, dt))
            qi = psb("dq_qi", [128, HF], I32)
            qf = psb("dq_qf", [128, HF], F32)
            scb = psb("dq_scb", [128, HF], F32)
            wob = psb("dq_wob", [128, HF], BF16)
            for k in range(8):
                dma("yd", qi[:], W["L0_ffn_fc1_q"][k * 128:(k + 1) * 128, :],
                    (), ("dq.qi",))
                vcopy(qf[:], qi[:], ("dq.qi",), ("dq.qf",))
                for h in range(8):
                    dma("yd", rowst[0:1, 0:512],
                        W["L0_ffn_fc1_scales"][k, h * 512:(h + 1) * 512][None, :],
                        (), ("rowst",))
                    mm(psum[6][:, 0:512], ones_row_f[:], rowst[0:1, 0:512],
                       True, True, ("ones_row_f", "rowst"), (BK[6],))
                    vcopy(scb[:, h * 512:(h + 1) * 512], psum[6][:, 0:512],
                          (BK[6],), ("dq.scb",))
                ts(qf[:], qf[:], -128.0, None, ALU.add, None, ("dq.qf",), ("dq.qf",))
                tt(wob[:], qf[:], scb[:], ALU.mult, ("dq.qf", "dq.scb"), ("dq.wob",))
                dma("gd", w1d[k * 128:(k + 1) * 128, :], wob[:], ("dq.wob",),
                    (f"w1d{k}",))
            for k in range(32):
                dma("yd", qi[:, 0:DM], W["L0_ffn_fc2_q"][k * 128:(k + 1) * 128, :],
                    (), ("dq.qi",))
                vcopy(qf[:, 0:DM], qi[:, 0:DM], ("dq.qi",), ("dq.qf",))
                for h in range(2):
                    dma("yd", rowst[0:1, 0:512],
                        W["L0_ffn_fc2_scales"][k, h * 512:(h + 1) * 512][None, :],
                        (), ("rowst",))
                    mm(psum[6][:, 0:512], ones_row_f[:], rowst[0:1, 0:512],
                       True, True, ("ones_row_f", "rowst"), (BK[6],))
                    vcopy(scb[:, h * 512:(h + 1) * 512], psum[6][:, 0:512],
                          (BK[6],), ("dq.scb",))
                ts(qf[:, 0:DM], qf[:, 0:DM], -128.0, None, ALU.add, None,
                   ("dq.qf",), ("dq.qf",))
                tt(wob[:, 0:DM], qf[:, 0:DM], scb[:, 0:DM], ALU.mult,
                   ("dq.qf", "dq.scb"), ("dq.wob",))
                dma("gd", w2d[k * 128:(k + 1) * 128, :], wob[:, 0:DM], ("dq.wob",),
                    (f"w2d{k}",))
        P.barrier()

        with contextlib.ExitStack() as stk:
            def psb(name, shape, dt):
                return stk.enter_context(nc.sbuf_tensor(name, shape, dt))
            wt = psb("bq_wt", [128, HF], F32)
            qf = psb("bq_qf", [128, HF], F32)
            q8 = psb("bq_q8", [128, HF], I8)
            qb = psb("bq_qb", [128, HF], BF16)

            def quant_weight(src, dst, R, C, sw_col, mu_col, mu_scale, tag):
                nt, nb = R // 128, C // 128
                acc = colw[:, 0:1]
                tmpc = colw[:, 1:2]
                for it in range(nt):
                    dma("yd", wt[:, 0:C], src[it * 128:(it + 1) * 128, :],
                        (), (tag + ".wt",))
                    if it == 0:
                        vred(acc, wt[:, 0:C], ALU.add, True, (tag + ".wt",),
                             (tag + ".acc",))
                    else:
                        vred(tmpc, wt[:, 0:C], ALU.add, True, (tag + ".wt",),
                             (tag + ".tc",))
                        tt(acc, acc, tmpc, ALU.add, (tag + ".acc", tag + ".tc"),
                           (tag + ".acc",))
                mm(psum[6][0:1, 0:1], acc, ones_col_f[:], True, True,
                   (tag + ".acc", "ones_col_f"), (BK[6],))
                ts(rowst[0:1, 0:1], psum[6][0:1, 0:1], 1.0 / (R * C), None,
                   ALU.mult, None, (BK[6],), (tag + ".mu",))
                ts(rowst[0:1, 1:2], rowst[0:1, 0:1], 1e-5, None, ALU.max, None,
                   (tag + ".mu",), (tag + ".mu2",))
                vrecip(rowst[0:1, 2:3], rowst[0:1, 1:2], (tag + ".mu2",),
                       (tag + ".sw",))
                mm(psum[6][:, 0:1], ones_row_f[:], rowst[0:1, 2:3], True, True,
                   ("ones_row_f", tag + ".sw"), (BK[6],))
                vcopy(sw_col[:, 0:1], psum[6][:, 0:1], (BK[6],), (tag + ".swc",))
                mm(psum[6][:, 0:1], ones_row_f[:], rowst[0:1, 0:1], True, True,
                   ("ones_row_f", tag + ".mu"), (BK[6],))
                if mu_scale == 1.0:
                    vcopy(mu_col[:, 0:1], psum[6][:, 0:1], (BK[6],), (tag + ".muc",))
                else:
                    ts(mu_col[:, 0:1], psum[6][:, 0:1], mu_scale, None, ALU.mult,
                       None, (BK[6],), (tag + ".muc",))
                stg = [psb(f"{tag}_stg{j}", [128, R], FP8) for j in range(nb)]
                for it in range(nt):
                    dma("yd", wt[:, 0:C], src[it * 128:(it + 1) * 128, :],
                        (), (tag + ".wt",))
                    act(qf[:, 0:C], wt[:, 0:C], AF.Identity, (tag + ".wt", tag + ".swc"),
                        (tag + ".qf",), scale=sw_col[:, 0:1])
                    ts(qf[:, 0:C], qf[:, 0:C], -1.4999999, 1.4999999, ALU.max,
                       ALU.min, (tag + ".qf",), (tag + ".qf",))
                    vcopy(q8[:, 0:C], qf[:, 0:C], (tag + ".qf",), (tag + ".q8",))
                    vcopy(qb[:, 0:C], q8[:, 0:C], (tag + ".q8",), (tag + ".qb",))
                    for jb in range(nb):
                        tr(psum[7][:, 0:128], qb[:, jb * 128:(jb + 1) * 128],
                           (tag + ".qb", "ident"), (BK[7],))
                        scopy(stg[jb][:, it * 128:(it + 1) * 128],
                              psum[7][:, 0:128], (BK[7],), (tag + f".stg{jb}",))
                for jb in range(nb):
                    dma("gd", dst[jb * 128:(jb + 1) * 128, :], stg[jb][:],
                        (tag + f".stg{jb}",), (tag + f".d{jb}",))

            quant_weight(W["L1_ffn_w_gate"], qgTd, HF, DM, swg_c, mug_c, 1.0, "qg")
            quant_weight(W["L1_ffn_w_up"], quTd, HF, DM, swu_c, muu_c, 1.0, "qu")
            quant_weight(W["L1_ffn_w_down"], qdTd, DM, HF, swd_c, mud_h, 0.5, "qd")
        P.barrier()

        def ln_stats(xq, scrap, q, xname, scrname="lnscrap"):
            if hasattr(xq, "ap") and not isinstance(xq, bass.AP):
                xq = xq[:]
            sx = colw[:, 8 * q + 0:8 * q + 1]
            sq = colw[:, 8 * q + 1:8 * q + 2]
            mu = colw[:, 8 * q + 2:8 * q + 3]
            var = colw[:, 8 * q + 3:8 * q + 4]
            vred(sx, xq, ALU.add, False, (xname,), (f"cw{q}.sx",))
            sqb = colw[:, 8 * q + 5:8 * q + 6]
            P.op("s", lambda e, _o=scrap[:, 0:512], _i=xq[:, 0:512], _a=sq:
                 e.activation(_o, _i, AF.Square, accum_out=_a),
                 (xname,), (f"cw{q}.sq", scrname))
            P.op("s", lambda e, _o=scrap[:, 0:512], _i=xq[:, 512:1024], _a=sqb:
                 e.activation(_o, _i, AF.Square, accum_out=_a),
                 (xname,), (f"cw{q}.sqb", scrname))
            tt(sq, sq, sqb, ALU.add, (f"cw{q}.sq", f"cw{q}.sqb"), (f"cw{q}.sq",))
            ts(mu, sx, 1.0 / DM, None, ALU.mult, None, (f"cw{q}.sx",), (f"cw{q}.mu",))
            ts(var, sq, 1.0 / DM, None, ALU.mult, None, (f"cw{q}.sq",), (f"cw{q}.var",))
            ts(sqb, mu, mu, None, ALU.mult, None, (f"cw{q}.mu",), (f"cw{q}.sqb",))
            tt(var, var, sqb, ALU.subtract,
               (f"cw{q}.var", f"cw{q}.sqb"), (f"cw{q}.var",))
            r1 = colw[:, 8 * q + 4:8 * q + 5]
            act(r1, var, AF.Ln, (f"cw{q}.var",), (f"cw{q}.ra",), bias=epsc[:, 0:1])
            act(r1, r1, AF.Exp, (f"cw{q}.ra",), (f"cw{q}.r",), scale=nhalfc[:, 0:1])
            return mu, var, r1

        def inner_ln_cols(mu, var, r1, q):
            var2 = colw[:, 8 * q + 6:8 * q + 7]
            ts(var2, var, r1, None, ALU.mult, None,
               (f"cw{q}.var", f"cw{q}.r"), (f"cw{q}.v2a",))
            ts(var2, var2, r1, None, ALU.mult, None,
               (f"cw{q}.v2a", f"cw{q}.r"), (f"cw{q}.var2",))
            r2 = colw[:, 8 * q + 7:8 * q + 8]
            act(r2, var2, AF.Ln, (f"cw{q}.var2",), (f"cw{q}.r2a",), bias=epsc[:, 0:1])
            act(r2, r2, AF.Exp, (f"cw{q}.r2a",), (f"cw{q}.r2",), scale=nhalfc[:, 0:1])
            sstar = colw[:, 64 + 4 * q + 0:64 + 4 * q + 1]
            bstar = colw[:, 64 + 4 * q + 1:64 + 4 * q + 2]
            ts(sstar, r1, r2, None, ALU.mult, None,
               (f"cw{q}.r", f"cw{q}.r2"), (f"cw{q}.ss",))
            ts(bstar, mu, sstar, -1.0, ALU.mult, ALU.mult,
               (f"cw{q}.mu", f"cw{q}.ss"), (f"cw{q}.bs",))
            return sstar, bstar

        def transpose_tok_to_fm(src, src_name, fm, fm_pref, bank=7):
            for j in range(8):
                for q in range(4):
                    tr(psum[bank][:, q * 128:(q + 1) * 128],
                       src[:, q * 1024 + j * 128:q * 1024 + (j + 1) * 128],
                       (src_name, "ident"), (BK[bank],))
                scopy(fm[j][:], psum[bank][:, 0:512], (BK[bank],),
                      (f"{fm_pref}{j}",))

        def stage_l0ffn(src_dram, src_tag, dst_dram, dst_tag, sn):
            with contextlib.ExitStack() as stk:
                def psb(name, shape, dt):
                    return stk.enter_context(nc.sbuf_tensor(name, shape, dt))
                w1s = [psb(f"{sn}_w1s{k}", [128, HF], BF16) for k in range(8)]
                w2b = psb(f"{sn}_w2b", [128, 8192], BF16)
                hinb = psb(f"{sn}_hinb", [128, 4096], BF16)
                hinfm = [psb(f"{sn}_hf{k}", [128, 512], BF16) for k in range(8)]
                h1fm = [psb(f"{sn}_h1{i}", [128, 512], BF16) for i in range(32)]
                xt = psb(f"{sn}_xt", [128, 4096], F32)
                eq = psb(f"{sn}_eq", [128, DM], F32)
                z2 = psb(f"{sn}_z2", [128, 512], F32)
                xnew = psb(f"{sn}_xnew", [128, 4096], F32)
                for k in range(8):
                    dma("yd", w1s[k][:], w1d[k * 128:(k + 1) * 128, :],
                        (f"w1d{k}",), (f"{sn}.w1{k}",))
                for c in range(NCHUNK):
                    dma("yd", tile3(xt), chunk_ap(src_dram, c),
                        tuple(f"{src_tag}.{c}.{qs}" for qs in range(4)) +
                        (f"{src_tag}.{c}",), (f"{sn}.xt",))
                    for q in range(4):
                        mu, var, r1 = ln_stats(xt[:, q * DM:(q + 1) * DM], eq, q, f"{sn}.xt",
                                               f"{sn}.eq")
                        sstar, bstar = inner_ln_cols(mu, var, r1, q)
                        act(eq[:], xt[:, q * DM:(q + 1) * DM], AF.Identity,
                            (f"{sn}.xt", f"cw{q}.ss", f"cw{q}.bs"), (f"{sn}.eq",),
                            bias=bstar, scale=sstar)
                        tt(eq[:], eq[:], g0_bc[:], ALU.mult,
                           (f"{sn}.eq", "g0"), (f"{sn}.eq",))
                        tt(hinb[:, q * DM:(q + 1) * DM], eq[:], b0_bc[:], ALU.add,
                           (f"{sn}.eq", "b0"), (f"{sn}.hin",))
                    transpose_tok_to_fm(hinb[:], f"{sn}.hin", hinfm, f"{sn}.hf")
                    for i in range(32):
                        bk = i % 2
                        for k in range(8):
                            mm(psum[bk][:], w1s[k][:, i * 128:(i + 1) * 128],
                               hinfm[k][:], k == 0, k == 7,
                               (f"{sn}.w1{k}", f"{sn}.hf{k}"), (BK[bk],))
                        act(h1fm[i][:], psum[bk][:], AF.Silu,
                            (BK[bk], "fc1b"), (f"{sn}.h1{i}",),
                            bias=fc1b_fm[:, i:i + 1])
                    for qp in range(2):
                        for kg in range(8):
                            pb = kg % 2
                            dma("yd",
                                w2b[:, pb * 4096:(pb + 1) * 4096].rearrange(
                                    "p (a d) -> p a d", a=4),
                                w2d[kg * 512:(kg + 1) * 512, :].rearrange(
                                    "(a p) d -> p a d", p=128),
                                tuple(f"w2d{kg * 4 + kk}" for kk in range(4)),
                                (f"{sn}.w2b{pb}",))
                            for kk in range(4):
                                k = kg * 4 + kk
                                for qq in range(2):
                                    q = qp * 2 + qq
                                    for n in range(2):
                                        mm(psum[3 + qq * 2 + n][:, 0:512],
                                           h1fm[k][:, q * 128:(q + 1) * 128],
                                           w2b[:, pb * 4096 + kk * 1024 + n * 512:
                                               pb * 4096 + kk * 1024 + (n + 1) * 512],
                                           k == 0, k == 31,
                                           (f"{sn}.h1{k}", f"{sn}.w2b{pb}"),
                                           (BK[3 + qq * 2 + n],))
                        for qq in range(2):
                            q = qp * 2 + qq
                            r1 = colw[:, 8 * q + 4:8 * q + 5]
                            mu = colw[:, 8 * q + 2:8 * q + 3]
                            s2c = colw[:, 64 + 4 * q + 2:64 + 4 * q + 3]
                            b2c = colw[:, 64 + 4 * q + 3:64 + 4 * q + 4]
                            ts(s2c, r1, 0.5, 1.0, ALU.mult, ALU.add,
                               (f"cw{q}.r",), (f"cw{q}.s2",))
                            ts(b2c, mu, r1, -0.5, ALU.mult, ALU.mult,
                               (f"cw{q}.mu", f"cw{q}.r"), (f"cw{q}.b2",))
                            act(eq[:], xt[:, q * DM:(q + 1) * DM], AF.Identity,
                                (f"{sn}.xt", f"cw{q}.s2", f"cw{q}.b2"),
                                (f"{sn}.eq",), bias=b2c, scale=s2c)
                            for n in range(2):
                                pbk = 3 + qq * 2 + n
                                stt(z2[:], psum[pbk][:, 0:512], 0.25,
                                    fc2b_bc4[:, n * 512:(n + 1) * 512],
                                    ALU.mult, ALU.add,
                                    (BK[pbk], "fc2b"), (f"{sn}.z2",))
                                tt(xnew[:, q * DM + n * 512:q * DM + (n + 1) * 512],
                                   eq[:, n * 512:(n + 1) * 512], z2[:], ALU.add,
                                   (f"{sn}.eq", f"{sn}.z2"), (f"{sn}.xn",))
                    for qs in range(4):
                        dma("gd",
                            dst_dram[c * CH + qs * 128:c * CH + (qs + 1) * 128, :],
                            xnew[:, qs * DM:(qs + 1) * DM],
                            (f"{sn}.xn",), (f"{dst_tag}.{c}.{qs}",))

        def stage_deltanet(src_dram, src_tag, dst_dram, dst_tag):
            sn = "S2"
            with contextlib.ExitStack() as stk:
                def psb(name, shape, dt):
                    return stk.enter_context(nc.sbuf_tensor(name, shape, dt))
                xt = psb("s2_xt", [128, 4096], F32)
                vtb = psb("s2_vtb", [128, 4096], BF16)
                vfm = [psb(f"s2_vf{k}", [128, 512], BF16) for k in range(8)]
                xnew = psb("s2_xnew", [128, 4096], F32)
                qkvg = psb("s2_qkvg", [128, 512], F32)
                ab2 = psb("s2_ab", [128, 2], F32)
                sgab = psb("s2_sgab", [128, 2], F32)
                khat = psb("s2_khat", [128, 128], F32)
                scr = psb("s2_scr", [128, 128], F32)
                t1f = psb("s2_t1f", [128, 128], F32)
                zh = psb("s2_zh", [128, 512], F32)
                Bq = psb("s2_q", [128, 128], BF16)
                Bk = psb("s2_k", [128, 128], BF16)
                Bv = psb("s2_v", [128, 128], BF16)
                Bg = psb("s2_g", [128, 128], F32)
                kT = psb("s2_kT", [128, 128], BF16)
                qT = psb("s2_qT", [128, 128], BF16)
                ksym = psb("s2_ksym", [128, 128], F32)
                P2t = psb("s2_P2", [128, 128], BF16)
                Dm = psb("s2_Dm", [128, 128], F32)
                DmT = psb("s2_DmT", [128, 128], F32)
                zU = psb("s2_zU", [128, 128], F32)
                DmTEi = psb("s2_DmTEi", [128, 128], F32)
                NT = psb("s2_NT", [128, 128], BF16)
                Nn = psb("s2_Nn", [128, 128], BF16)
                Ga = psb("s2_Ga", [128, 128], BF16)
                Gb2 = psb("s2_Gb", [128, 128], BF16)
                Pk1 = psb("s2_Pk1", [128, 128], BF16)
                PkT1 = psb("s2_PkT1", [128, 128], BF16)
                Pk2 = psb("s2_Pk2", [128, 128], BF16)
                PkT2 = psb("s2_PkT2", [128, 128], BF16)
                U0 = psb("s2_U0", [128, 128], BF16)
                MT = psb("s2_MT", [128, 128], BF16)
                Ub = psb("s2_U", [128, 128], BF16)
                UT = psb("s2_UT", [128, 128], BF16)
                WTb = psb("s2_WT", [128, 128], BF16)
                KG = psb("s2_KG", [128, 128], BF16)
                wU = psb("s2_wU", [128, 128], BF16)
                OT = psb("s2_OT", [128, 128], BF16)
                Ob = psb("s2_Ob", [128, 128], BF16)
                scrap = psb("s2_lnscr", [128, DM], F32)
                for c in range(NCHUNK):
                    dma("yd", tile3(xt), chunk_ap(src_dram, c),
                        tuple(f"{src_tag}.{c}.{qs}" for qs in range(4)) +
                        (f"{src_tag}.{c}",), ("s2.xt",))
                    for q in range(4):
                        mu, var, r1 = ln_stats(xt[:, q * DM:(q + 1) * DM], scrap, q, "s2.xt",
                                               "s2.lnscr")
                        b1c = colw[:, 64 + 4 * q:64 + 4 * q + 1]
                        ts(b1c, mu, r1, -1.0, ALU.mult, ALU.mult,
                           (f"cw{q}.mu", f"cw{q}.r"), (f"cw{q}.b1",))
                        act(vtb[:, q * DM:(q + 1) * DM], xt[:, q * DM:(q + 1) * DM],
                            AF.Identity, ("s2.xt", f"cw{q}.r", f"cw{q}.b1"), ("s2.v",),
                            bias=b1c, scale=r1)
                    transpose_tok_to_fm(vtb[:], "s2.v", vfm, "s2.vf")
                    for tg in range(4):
                        for k in range(8):
                            mm(psum[0][:, 0:512], vfm[k][:, tg * 128:(tg + 1) * 128],
                               wqkvg[k][:], k == 0, k == 7,
                               (f"s2.vf{k}", f"wqkvg{k}"), (BK[0],))
                            mm(psum[1][:, 0:2], vfm[k][:, tg * 128:(tg + 1) * 128],
                               wab[k][:], k == 0, k == 7,
                               (f"s2.vf{k}", f"wab{k}"), (BK[1],))
                        tt(qkvg[:], psum[0][:, 0:512], bqkvg_bc[:], ALU.add,
                           (BK[0], "bqkvg"), ("s2.pj",))
                        tt(ab2[:], psum[1][:, 0:2], bab_bc[:], ALU.add,
                           (BK[1], "bab"), ("s2.ab2",))
                        act(sgab[:], ab2[:], AF.Sigmoid, ("s2.ab2",), ("s2.sgab",))
                        act(Bg[:], qkvg[:, 384:512], AF.Sigmoid, ("s2.pj",),
                            ("s2.gate",))
                        zc = colw[:, 80:81]
                        act(zc, sgab[:, 0:1], AF.Ln, ("s2.sgab",), ("s2.zc",))
                        n2 = colw[:, 81:82]
                        act(scr[:], qkvg[:, 128:256], AF.Square,
                            ("s2.pj",), ("s2.n2", "s2.scr"), accum=n2)
                        ts(n2, n2, 1e-24, None, ALU.max, None, ("s2.n2",), ("s2.n2",))
                        rn = colw[:, 82:83]
                        act(rn, n2, AF.Ln, ("s2.n2",), ("s2.rna",))
                        act(rn, rn, AF.Exp, ("s2.rna",), ("s2.rn",), scale=nhalfc[:, 0:1])
                        act(khat[:], qkvg[:, 128:256], AF.Identity,
                            ("s2.pj", "s2.rn"), ("s2.khat",), scale=rn)
                        vcopy(Bk[:], khat[:], ("s2.khat",), ("s2.k",))
                        vcopy(Bq[:], qkvg[:, 0:128], ("s2.pj",), ("s2.q",))
                        vcopy(Bv[:], qkvg[:, 256:384], ("s2.pj",), ("s2.vv",))
                        Lc = colw[:, 83:84]
                        mm(psum[1][:, 4:5], inclT_f[:], zc, True, True,
                           ("inclT", "s2.zc"), (BK[1],))
                        vcopy(Lc, psum[1][:, 4:5], (BK[1],), ("s2.L",))
                        gam = colw[:, 84:85]
                        act(gam, Lc, AF.Exp, ("s2.L",), ("s2.gam",))
                        mm(psum[1][0:1, 6:7], zc, ones_col_f[:], True, True,
                           ("s2.zc", "ones_col_f"), (BK[1],))
                        vcopy(rowst[0:1, 3:4], psum[1][0:1, 6:7], (BK[1],),
                              ("rowst",))
                        mm(psum[1][:, 5:6], ones_row_f[:], rowst[0:1, 3:4],
                           True, True, ("ones_row_f", "rowst"), (BK[1],))
                        LNb = colw[:, 85:86]
                        vcopy(LNb, psum[1][:, 5:6], (BK[1],), ("s2.LN",))
                        gamN = colw[:, 86:87]
                        act(gamN, LNb, AF.Exp, ("s2.LN",), ("s2.gamN",))
                        wv = colw[:, 87:88]
                        tt(wv, LNb, Lc, ALU.subtract, ("s2.LN", "s2.L"), ("s2.wva",))
                        act(wv, wv, AF.Exp, ("s2.wva",), ("s2.wvb",))
                        ts(wv, wv, sgab[:, 1:2], None, ALU.mult, None,
                           ("s2.wvb", "s2.sgab"), ("s2.wv",))
                        tr(psum[2][:, 0:128], Bk[:], ("s2.k", "ident"), (BK[2],))
                        scopy(kT[:], psum[2][:, 0:128], (BK[2],), ("s2.kT",))
                        tr(psum[2][:, 128:256], Bq[:], ("s2.q", "ident"), (BK[2],))
                        scopy(qT[:], psum[2][:, 128:256], (BK[2],), ("s2.qT",))
                        mm(psum[3][:, 0:128], kT[:], kT[:], True, True,
                           ("s2.kT",), (BK[3],))
                        scopy(ksym[:], psum[3][:, 0:128], (BK[3],), ("s2.ksym",))
                        act(P2t[:], ksym[:], AF.Identity, ("s2.ksym", "s2.sgab"),
                            ("s2.P2",), scale=sgab[:, 1:2])
                        ts(zU[:], maskLneg[:], zc, -1.0, ALU.mult, ALU.mult,
                           ("maskLneg", "s2.zc"), ("s2.zU",))
                        mm(psum[3][:, 128:256], zU[:], inclT_f[:], True, True,
                           ("s2.zU", "inclT"), (BK[3],))
                        ts(DmT[:], psum[3][:, 128:256], 0.0, None, ALU.min, None,
                           (BK[3],), ("s2.DmTa",))
                        act(DmT[:], DmT[:], AF.Exp, ("s2.DmTa",), ("s2.DmT",))
                        mm(psum[3][:, 256:384], inclT_f[:], zU[:], True, True,
                           ("s2.zU", "inclT"), (BK[3],))
                        ts(Dm[:], psum[3][:, 256:384], 0.0, None, ALU.min, None,
                           (BK[3],), ("s2.Dma",))
                        act(Dm[:], Dm[:], AF.Exp, ("s2.Dma",), ("s2.Dm",))
                        tt(t1f[:], DmT[:], P2t[:], ALU.mult,
                           ("s2.DmT", "s2.P2"), ("s2.t1",))
                        tt(NT[:], t1f[:], maskUneg[:], ALU.mult,
                           ("s2.t1", "maskUneg"), ("s2.NT",))
                        tr(psum[2][:, 256:384], NT[:], ("s2.NT", "ident"), (BK[2],))
                        scopy(Nn[:], psum[2][:, 256:384], (BK[2],), ("s2.Nn",))
                        tt(Ga[:], NT[:], ident_bf[:], ALU.add,
                           ("s2.NT", "ident"), ("s2.G",))
                        gc, gn = Ga, Gb2
                        pc, pTc = NT, Nn
                        pcn, pTcn = "s2.NT", "s2.Nn"
                        gcn = "s2.G"
                        for it in range(6):
                            pn = Pk1 if it % 2 == 0 else Pk2
                            pTn = PkT1 if it % 2 == 0 else PkT2
                            mm(psum[4][:, 0:128], pTc[:], pc[:], True, True,
                               (pcn, pTcn), (BK[4],))
                            scopy(pn[:], psum[4][:, 0:128], (BK[4],),
                                  (f"s2.P{it % 2}",))
                            mm(psum[4][:, 128:256], pc[:], pTc[:], True, True,
                               (pcn, pTcn), (BK[4],))
                            scopy(pTn[:], psum[4][:, 128:256], (BK[4],),
                                  (f"s2.PT{it % 2}",))
                            mm(psum[5][:, 0:128], pTn[:], gc[:], True, True,
                               (f"s2.PT{it % 2}", gcn), (BK[5],))
                            tt(gn[:], psum[5][:, 0:128], gc[:], ALU.add,
                               (BK[5], gcn), ("s2.G2" if gn is Gb2 else "s2.G",))
                            gcn = "s2.G2" if gn is Gb2 else "s2.G"
                            gc, gn = gn, gc
                            pc, pTc = pn, pTn
                            pcn, pTcn = f"s2.P{it % 2}", f"s2.PT{it % 2}"
                        Gf, Gfn = gc, gcn
                        mm(psum[4][:, 256:384], Gf[:], Bv[:], True, True,
                           (Gfn, "s2.vv"), (BK[4],))
                        scopy(U0[:], psum[4][:, 256:384], (BK[4],), ("s2.U0",))
                        act(KG[:], khat[:], AF.Identity, ("s2.khat", "s2.gam"),
                            ("s2.KG",), scale=gam)
                        mm(psum[4][:, 384:512], KG[:], Gf[:], True, True,
                           ("s2.KG", Gfn), (BK[4],))
                        scopy(MT[:], psum[4][:, 384:512], (BK[4],), ("s2.MT",))
                        mm(psum[5][:, 128:256], MT[:], ST_bf[:], True, True,
                           ("s2.MT", "ST_bf"), (BK[5],))
                        tt(Ub[:], U0[:], psum[5][:, 128:256], ALU.subtract,
                           ("s2.U0", BK[5]), ("s2.U",))
                        tr(psum[2][:, 384:512], Ub[:], ("s2.U", "ident"), (BK[2],))
                        scopy(UT[:], psum[2][:, 384:512], (BK[2],), ("s2.UT",))
                        tt(DmTEi[:], DmT[:], inclT_f[:], ALU.mult,
                           ("s2.DmT", "inclT"), ("s2.DmTEi",))
                        mm(psum[5][:, 256:384], UT[:], qT[:], True, True,
                           ("s2.UT", "s2.qT"), (BK[5],))
                        tt(t1f[:], psum[5][:, 256:384], DmTEi[:], ALU.mult,
                           (BK[5], "s2.DmTEi"), ("s2.wt1",))
                        act(WTb[:], t1f[:], AF.Identity, ("s2.wt1", "s2.sgab"),
                            ("s2.WTb",), scale=sgab[:, 1:2])
                        mm(psum[6][:, 0:128], qT[:], S_bf[:], True, True,
                           ("s2.qT", "S_bf"), (BK[6],))
                        act(t1f[:], psum[6][:, 0:128], AF.Identity,
                            (BK[6], "s2.gam"), ("s2.O1",), scale=gam)
                        mm(psum[6][:, 128:256], WTb[:], Bk[:], True, True,
                           ("s2.WTb", "s2.k"), (BK[6],))
                        tt(t1f[:], t1f[:], psum[6][:, 128:256], ALU.add,
                           ("s2.O1", BK[6]), ("s2.O2",))
                        tt(Ob[:], t1f[:], Bg[:], ALU.mult,
                           ("s2.O2", "s2.gate"), ("s2.O",))
                        act(wU[:], Ub[:], AF.Identity, ("s2.U", "s2.wv"), ("s2.wU",),
                            scale=wv)
                        mm(psum[6][:, 256:384], wU[:], Bk[:], True, True,
                           ("s2.wU", "s2.k"), (BK[6],))
                        stt(Sst[:], Sst[:], gamN, psum[6][:, 256:384],
                            ALU.mult, ALU.add, ("Sst", "s2.gamN", BK[6]), ("Sst",))
                        mm(psum[6][:, 384:512], Bk[:], wU[:], True, True,
                           ("s2.wU", "s2.k"), (BK[6],))
                        stt(STst[:], STst[:], gamN, psum[6][:, 384:512],
                            ALU.mult, ALU.add, ("STst", "s2.gamN", BK[6]), ("STst",))
                        vcopy(S_bf[:], Sst[:], ("Sst",), ("S_bf",))
                        vcopy(ST_bf[:], STst[:], ("STst",), ("ST_bf",))
                        tr(psum[2][:, 0:128], Ob[:], ("s2.O", "ident"), (BK[2],))
                        scopy(OT[:], psum[2][:, 0:128], (BK[2],), ("s2.OT",))
                        for n in range(2):
                            mm(psum[3][:, 384:512] if False else psum[3][:, 0:512],
                               OT[:], wo_b[:, n * 512:(n + 1) * 512], True, True,
                               ("s2.OT", "wo_b"), (BK[3],))
                            tt(zh[:], psum[3][:, 0:512],
                               bo_bc[:, n * 512:(n + 1) * 512], ALU.add,
                               (BK[3], "bo"), ("s2.zh",))
                            tt(xnew[:, tg * DM + n * 512:tg * DM + (n + 1) * 512],
                               zh[:], xt[:, tg * DM + n * 512:tg * DM + (n + 1) * 512],
                               ALU.add, ("s2.zh", "s2.xt"), ("s2.xn",))
                    for qs in range(4):
                        dma("gd",
                            dst_dram[c * CH + qs * 128:c * CH + (qs + 1) * 128, :],
                            xnew[:, qs * DM:(qs + 1) * DM],
                            ("s2.xn",), (f"{dst_tag}.{c}.{qs}",))

        def stage_bitlin(src_dram, src_tag, dst_dram, dst_tag, sn):
            with contextlib.ExitStack() as stk:
                def psb(name, shape, dt):
                    return stk.enter_context(nc.sbuf_tensor(name, shape, dt))
                qgs = [psb(f"{sn}_qg{k}", [128, HF], FP8) for k in range(8)]
                qus = [psb(f"{sn}_qu{k}", [128, HF], FP8) for k in range(8)]
                qds = [psb(f"{sn}_qd{k}", [128, DM], FP8) for k in range(32)]
                xt = psb(f"{sn}_xt", [128, 4096], F32)
                tok = psb(f"{sn}_tok", [128, 4096], BF16)
                ht = psb(f"{sn}_ht", [128, 4096], BF16)
                qxfm = [psb(f"{sn}_qx{k}", [128, 512], BF16) for k in range(8)]
                qhfm = psb(f"{sn}_qhfm", [128, 4096], BF16)
                sc1 = psb(f"{sn}_sc1", [128, 512], BF16)
                sc2 = psb(f"{sn}_sc2", [128, 512], BF16)
                i8s = psb(f"{sn}_i8", [128, DM], I8)
                vq = psb(f"{sn}_vq", [128, DM], F32)
                zd = psb(f"{sn}_zd", [128, 512], F32)
                for k in range(8):
                    dma("yd", qgs[k][:], qgTd[k * 128:(k + 1) * 128, :],
                        (f"qg.d{k}",), (f"{sn}.qg{k}",))
                    dma("yd", qus[k][:], quTd[k * 128:(k + 1) * 128, :],
                        (f"qu.d{k}",), (f"{sn}.qu{k}",))
                for k in range(32):
                    dma("yd", qds[k][:], qdTd[k * 128:(k + 1) * 128, :],
                        (f"qd.d{k}",), (f"{sn}.qd{k}",))
                for c in range(NCHUNK):
                    dma("yd", tile3(xt), chunk_ap(src_dram, c),
                        tuple(f"{src_tag}.{c}.{qs}" for qs in range(4)) +
                        (f"{src_tag}.{c}",), (f"{sn}.xt",))
                    for q in range(4):
                        mu, var, r1 = ln_stats(xt[:, q * DM:(q + 1) * DM], vq, q, f"{sn}.xt",
                                               f"{sn}.vq")
                        b1c = colw[:, 64 + 4 * q:64 + 4 * q + 1]
                        ts(b1c, mu, r1, -1.0, ALU.mult, ALU.mult,
                           (f"cw{q}.mu", f"cw{q}.r"), (f"cw{q}.b1",))
                        act(vq[:], xt[:, q * DM:(q + 1) * DM], AF.Identity,
                            (f"{sn}.xt", f"cw{q}.r", f"cw{q}.b1"), (f"{sn}.vq",),
                            bias=b1c, scale=r1)
                        m = colw[:, 64 + 4 * q + 1:64 + 4 * q + 2]
                        vred(m, vq[:], ALU.max, True, (f"{sn}.vq",), (f"cw{q}.m",))
                        ts(m, m, 1e-5, None, ALU.max, None, (f"cw{q}.m",),
                           (f"cw{q}.m",))
                        im = colw[:, 64 + 4 * q + 2:64 + 4 * q + 3]
                        vrecip(im, m, (f"cw{q}.m",), (f"cw{q}.im",))
                        sxc = colw[:, 64 + 4 * q + 3:64 + 4 * q + 4]
                        ts(sxc, im, 127.0, None, ALU.mult, None, (f"cw{q}.im",),
                           (f"cw{q}.sx",))
                        isx = colw[:, 88 + q:89 + q]
                        ts(isx, m, 1.0 / 127.0, None, ALU.mult, None,
                           (f"cw{q}.m",), (f"cw{q}.isx",))
                        act(vq[:], vq[:], AF.Identity, (f"{sn}.vq", f"cw{q}.sx"),
                            (f"{sn}.vq2",), scale=sxc)
                        vcopy(i8s[:], vq[:], (f"{sn}.vq2",), (f"{sn}.i8",))
                        vcopy(tok[:, q * DM:(q + 1) * DM], i8s[:], (f"{sn}.i8",),
                              (f"{sn}.tok",))
                    transpose_tok_to_fm(tok[:], f"{sn}.tok", qxfm, f"{sn}.qf")
                    for q in range(4):
                        isx = colw[:, 88 + q:89 + q]
                        gsc = colw[:, 92:93]
                        usc = colw[:, 93:94]
                        ts(gsc, isx, mug_c[:, 0:1], None, ALU.mult, None,
                           (f"cw{q}.isx", "qg.muc"), (f"{sn}.gs",))
                        ts(usc, isx, muu_c[:, 0:1], None, ALU.mult, None,
                           (f"cw{q}.isx", "qu.muc"), (f"{sn}.us",))
                        for iq in range(4):
                            for k in range(8):
                                lhs = qxfm[k][:, q * 128:(q + 1) * 128]
                                for n in range(2):
                                    mm(psum[n][:, 0:512], lhs,
                                       qgs[k][:, iq * 1024 + n * 512:
                                               iq * 1024 + (n + 1) * 512],
                                       k == 0, k == 7,
                                       (f"{sn}.qf{k}", f"{sn}.qg{k}"), (BK[n],))
                                for n in range(2):
                                    mm(psum[3 + n][:, 0:512], lhs,
                                       qus[k][:, iq * 1024 + n * 512:
                                               iq * 1024 + (n + 1) * 512],
                                       k == 0, k == 7,
                                       (f"{sn}.qf{k}", f"{sn}.qu{k}"), (BK[3 + n],))
                            for n in range(2):
                                act(sc1[:], psum[n][:, 0:512], AF.Relu,
                                    (BK[n], f"{sn}.gs"), (f"{sn}.rg",), scale=gsc)
                                tt(sc2[:], sc1[:], sc1[:], ALU.mult,
                                   (f"{sn}.rg",), (f"{sn}.rg2",))
                                stt(ht[:, iq * 1024 + n * 512:
                                       iq * 1024 + (n + 1) * 512],
                                    sc2[:], usc, psum[3 + n][:, 0:512],
                                    ALU.mult, ALU.mult,
                                    (f"{sn}.rg2", f"{sn}.us", BK[3 + n]),
                                    (f"{sn}.h",))
                        mh = colw[:, 92:93]
                        vred(mh, ht[:], ALU.max, True, (f"{sn}.h",), (f"{sn}.mh",))
                        ts(mh, mh, 1e-5, None, ALU.max, None, (f"{sn}.mh",),
                           (f"{sn}.mh",))
                        imh = colw[:, 93:94]
                        vrecip(imh, mh, (f"{sn}.mh",), (f"{sn}.imh",))
                        sxh = colw[:, 94:95]
                        ts(sxh, imh, 127.0, None, ALU.mult, None, (f"{sn}.imh",),
                           (f"{sn}.sxh",))
                        isxh = colw[:, 95:96]
                        ts(isxh, mh, 1.0 / 127.0, None, ALU.mult, None,
                           (f"{sn}.mh",), (f"{sn}.isxh",))
                        for hsl in range(8):
                            sl = slice(hsl * 512, (hsl + 1) * 512)
                            act(zd[:], ht[:, sl], AF.Identity,
                                (f"{sn}.h", f"{sn}.sxh"), (f"{sn}.zd",), scale=sxh)
                            vcopy(i8s[:, 0:512], zd[:], (f"{sn}.zd",),
                                  (f"{sn}.i8",))
                            vcopy(tok[:, sl], i8s[:, 0:512], (f"{sn}.i8",),
                                  (f"{sn}.tok",))
                        for jb in range(8):
                            for kk in range(4):
                                k = jb * 4 + kk
                                tr(psum[7][:, kk * 128:(kk + 1) * 128],
                                   tok[:, k * 128:(k + 1) * 128],
                                   (f"{sn}.tok", "ident"), (BK[7],))
                            scopy(qhfm[:, jb * 512:(jb + 1) * 512],
                                  psum[7][:, 0:512], (BK[7],), (f"{sn}.qhf",))
                        for k in range(32):
                            for n in range(2):
                                mm(psum[n][:, 0:512],
                                   qhfm[:, k * 128:(k + 1) * 128],
                                   qds[k][:, n * 512:(n + 1) * 512],
                                   k == 0, k == 31,
                                   (f"{sn}.qhf", f"{sn}.qd{k}"), (BK[n],))
                        rs = colw[:, 92:93]
                        ts(rs, isxh, mud_h[:, 0:1], None, ALU.mult, None,
                           (f"{sn}.isxh", "qd.muc"), (f"{sn}.rs",))
                        for n in range(2):
                            act(zd[:], psum[n][:, 0:512], AF.Identity,
                                (BK[n], f"{sn}.rs"), (f"{sn}.zd",), scale=rs)
                            tt(vq[:, n * 512:(n + 1) * 512],
                               zd[:], xt[:, q * DM + n * 512:q * DM + (n + 1) * 512],
                               ALU.add, (f"{sn}.zd", f"{sn}.xt"), (f"{sn}.vq",))
                        dma("gd", dst_dram[c * CH + q * 128:c * CH + (q + 1) * 128, :],
                            vq[:], (f"{sn}.vq",), (f"{dst_tag}.{c}.{q}",))

        def stage_mamba(src_dram, src_tag, dst_dram, dst_tag):
            sn = "S5"
            with contextlib.ExitStack() as stk:
                def psb(name, shape, dt):
                    return stk.enter_context(nc.sbuf_tensor(name, shape, dt))
                winb = [psb(f"s5_wi{k}", [128, HF], BF16) for k in range(8)]
                wob = psb("s5_wob", [128, 4096], BF16)
                xh_ext = psb("s5_xhe", [128, 515], BF16)
                xnb = psb("s5_xnb", [128, 4096], BF16)
                xnfm = [psb(f"s5_xf{k}", [128, 512], BF16) for k in range(8)]
                dth = [psb(f"s5_dt{d}", [128, 512], BF16) for d in range(8)]
                ugh = [psb(f"s5_ug{d}", [128, 512], BF16) for d in range(8)]
                xcall = [psb(f"s5_xc{d}", [128, 512], BF16) for d in range(16)]
                gtall = [psb(f"s5_gt{d}", [128, 512], BF16) for d in range(16)]
                zstg = psb("s5_zstg", [128, 512], BF16)
                Bbc = [psb("s5_Bbc0", [128, 512], BF16)] * 2
                Cbc = [psb("s5_Cbc0", [128, 512], BF16)] * 2
                wscr = psb("s5_wscr", [128, 512], BF16)
                gscr = psb("s5_gscr", [128, 512], BF16)
                yall = [psb(f"s5_y{d}", [128, 512], F32) for d in range(8)]
                xw = psb("s5_xw", [128, DM], F32)
                dAc = psb("s5_dA", [128, 512], F32)
                cs1 = psb("s5_cs1", [128, 512], F32)
                xpj = psb("s5_xpj", [33, 512], F32)
                cs2 = dAc
                dtrb = cs1
                dtp = dAc
                for k in range(8):
                    dma("gd", winb[k][:], W["L1_attn_W_in"][k * 128:(k + 1) * 128, :],
                        (), (f"s5.wi{k}",))
                for c in range(NCHUNK):
                    for q in range(4):
                        dma("yd", xw[:],
                            src_dram[c * CH + q * 128:c * CH + (q + 1) * 128, :],
                            (f"{src_tag}.{c}.{q}", f"{src_tag}.{c}"), ("s5.xw",))
                        mu, var, r1 = ln_stats(xw, cs1, q, "s5.xw", "s5.cs1")
                        sstar, bstar = inner_ln_cols(mu, var, r1, q)
                        for hh in range(2):
                            hsl = slice(hh * 512, (hh + 1) * 512)
                            act(cs1[:], xw[:, hsl], AF.Identity,
                                ("s5.xw", f"cw{q}.ss", f"cw{q}.bs"), ("s5.cs1",),
                                bias=bstar, scale=sstar)
                            tt(cs1[:], cs1[:], g1_bc[:, hsl], ALU.mult,
                               ("s5.cs1", "g1"), ("s5.cs1",))
                            tt(xnb[:, q * DM + hh * 512:q * DM + (hh + 1) * 512],
                               cs1[:], b1_bc[:, hsl], ALU.add,
                               ("s5.cs1", "b1"), ("s5.xnb",))
                        s3 = colw[:, 64 + 4 * q + 2:64 + 4 * q + 3]
                        b3 = colw[:, 64 + 4 * q + 3:64 + 4 * q + 4]
                        ts(s3, r1, 1.0, None, ALU.add, None, (f"cw{q}.r",),
                           (f"cw{q}.s3",))
                        ts(b3, mu, r1, -1.0, ALU.mult, ALU.mult,
                           (f"cw{q}.mu", f"cw{q}.r"), (f"cw{q}.b3",))
                        act(xw[:], xw[:],
                            AF.Identity, ("s5.xw", f"cw{q}.s3", f"cw{q}.b3"),
                            ("s5.xw",), bias=b3, scale=s3)
                        dma("gd", xC[c * CH + q * 128:c * CH + (q + 1) * 128, :],
                            xw[:], ("s5.xw",), (f"xC.{c}.{q}",))
                    transpose_tok_to_fm(xnb[:], "s5.xnb", xnfm, "s5.xf")
                    for di in range(32):
                        bk = di % 2
                        for k in range(8):
                            mm(psum[bk][:], winb[k][:, di * 128:(di + 1) * 128],
                               xnfm[k][:], k == 0, k == 7,
                               (f"s5.wi{k}", f"s5.xf{k}"), (BK[bk],))
                        if di < 16:
                            vcopy(xh_ext[:, 0:3], halo[:, di * 3:(di + 1) * 3],
                                  (f"halo{di}",), ("s5.xhe",))
                            scopy(xh_ext[:, 3:515], psum[bk][:], (BK[bk],),
                                  ("s5.xhe",))
                            vcopy(halo[:, di * 3:(di + 1) * 3], xh_ext[:, 512:515],
                                  ("s5.xhe",), (f"halo{di}",))
                            act(cs1[:], xh_ext[:, 0:512], AF.Identity,
                                ("s5.xhe", "convw", "convb"), ("s5.cs1",),
                                bias=convb[:, di:di + 1],
                                scale=convw[:, di * 4:di * 4 + 1])
                            stt(cs2[:], xh_ext[:, 1:513],
                                convw[:, di * 4 + 1:di * 4 + 2], cs1[:],
                                ALU.mult, ALU.add,
                                ("s5.xhe", "convw", "s5.cs1"), ("s5.dA",))
                            stt(cs1[:], xh_ext[:, 2:514],
                                convw[:, di * 4 + 2:di * 4 + 3], cs2[:],
                                ALU.mult, ALU.add,
                                ("s5.xhe", "convw", "s5.dA"), ("s5.cs1",))
                            stt(cs2[:], xh_ext[:, 3:515],
                                convw[:, di * 4 + 3:di * 4 + 4], cs1[:],
                                ALU.mult, ALU.add,
                                ("s5.xhe", "convw", "s5.cs1"), ("s5.dA",))
                            act(xcall[di][:], cs2[:], AF.Silu, ("s5.dA",),
                                (f"s5.xc{di}",))
                        else:
                            d2 = di - 16
                            act(zstg[:], psum[bk][:], AF.Silu, (BK[bk],),
                                ("s5.zstg",))
                            dma("gd", zsd[d2 * 128:(d2 + 1) * 128,
                                         c * CH:(c + 1) * CH], zstg[:],
                                ("s5.zstg",), (f"zsd.{c}.{d2}",))
                    for k in range(16):
                        mm(psum[6][0:33, 0:512], wxp[k][:], xcall[k][:],
                           k == 0, k == 15, (f"wxp{k}", f"s5.xc{k}"), (BK[6],))
                    scopy(xpj[:], psum[6][0:33, 0:512], (BK[6],), ("s5.xpj",))
                    dma("gd", xpjd, xpj[:], ("s5.xpj",), ("xpjd",))
                    dma("yd", dtrb[:], xpjd[32:33, :].broadcast_to([128, 512]),
                        ("xpjd",), ("s5.cs1",))
                    for hp in range(2):
                        for d2 in range(8):
                            di = hp * 8 + d2
                            ts(dtp[:], dtrb[:], wdt_c[:, di:di + 1],
                               bdt_c[:, di:di + 1], ALU.mult, ALU.add,
                               ("s5.cs1", "wdt", "bdt"), ("s5.dA",))
                            act(dtp[:], dtp[:], AF.Exp, ("s5.dA",), ("s5.dA",))
                            act(dth[d2][:], dtp[:], AF.Ln, ("s5.dA",),
                                (f"s5.dt{d2}",), bias=onec[:, 0:1])
                            tt(ugh[d2][:], dth[d2][:], xcall[di][:], ALU.mult,
                               (f"s5.dt{d2}", f"s5.xc{di}"), (f"s5.u{d2}",))
                            memset(yall[d2][:], 0.0, (f"s5.y{d2}",))
                        for n in range(DS):
                            pb = n % 2
                            dma("gd", Bbc[pb][:],
                                xpjd[n:n + 1, :].broadcast_to([128, 512]),
                                ("xpjd",), ("s5.Bbc0",))
                            dma("gd", Cbc[pb][:],
                                xpjd[16 + n:17 + n, :].broadcast_to([128, 512]),
                                ("xpjd",), ("s5.Cbc0",))
                            for d2 in range(8):
                                di = hp * 8 + d2
                                act(dAc[:], dth[d2][:], AF.Exp,
                                    (f"s5.dt{d2}", "Ac"), ("s5.dA",),
                                    scale=A_c[:, di * DS + n:di * DS + n + 1])
                                tt(wscr[:], ugh[d2][:], Bbc[pb][:], ALU.mult,
                                   (f"s5.u{d2}", "s5.Bbc0"), ("s5.w",))
                                P.op("v", lambda e, _o=gscr[:], _a=dAc[:],
                                     _b=wscr[:],
                                     _h=hstate[:, di * DS + n:di * DS + n + 1]:
                                     e.tensor_tensor_scan(_o, _a, _b, _h,
                                                          ALU.mult, ALU.add),
                                     ("s5.dA", "s5.w", f"hs{di}"), ("s5.g",))
                                vcopy(hstate[:, di * DS + n:di * DS + n + 1],
                                      gscr[:, 511:512], ("s5.g",), (f"hs{di}",))
                                tt(wscr[:], gscr[:], Cbc[pb][:], ALU.mult,
                                   ("s5.g", "s5.Cbc0"), ("s5.cg",))
                                tt(yall[d2][:], yall[d2][:], wscr[:], ALU.add,
                                   (f"s5.y{d2}", "s5.cg"), (f"s5.y{d2}",))
                        for d2 in range(8):
                            di = hp * 8 + d2
                            stt(yall[d2][:], xcall[di][:], D_c[:, di:di + 1],
                                yall[d2][:], ALU.mult, ALU.add,
                                (f"s5.xc{di}", "Dc", f"s5.y{d2}"), (f"s5.y{d2}",))
                            dma("yd", zstg[:], zsd[di * 128:(di + 1) * 128,
                                                   c * CH:(c + 1) * CH],
                                (f"zsd.{c}.{di}",), ("s5.zld",))
                            tt(gtall[di][:], yall[d2][:], zstg[:], ALU.mult,
                               (f"s5.y{d2}", "s5.zld"), (f"s5.ga{di}",))
                    for qp in range(2):
                        for kg in range(4):
                            dma("gd", wob[:].rearrange("p (a d) -> p a d", a=4),
                                W["L1_attn_W_out"][kg * 512:(kg + 1) * 512, :]
                                .rearrange("(a p) d -> p a d", p=128), (),
                                ("s5.wob",))
                            for kk in range(4):
                                k = kg * 4 + kk
                                for qq in range(2):
                                    q = qp * 2 + qq
                                    for n in range(2):
                                        bkx = (0, 1, 3, 4)[qq * 2 + n]
                                        mm(psum[bkx][:, 0:512],
                                           gtall[k][:, q * 128:(q + 1) * 128],
                                           wob[:, kk * 1024 + n * 512:
                                               kk * 1024 + (n + 1) * 512],
                                           k == 0, k == 15,
                                           (f"s5.ga{k}", "s5.wob"),
                                           (BK[bkx],))
                        for qq in range(2):
                            q = qp * 2 + qq
                            dma("yd", xw[:],
                                xC[c * CH + q * 128:c * CH + (q + 1) * 128, :],
                                (f"xC.{c}.{q}",), ("s5.xw",))
                            for n in range(2):
                                bkx = (0, 1, 3, 4)[qq * 2 + n]
                                tt(xw[:, n * 512:(n + 1) * 512],
                                   psum[bkx][:, 0:512],
                                   xw[:, n * 512:(n + 1) * 512],
                                   ALU.add, (BK[bkx], "s5.xw"), ("s5.xw",))
                            dma("gd",
                                dst_dram[c * CH + q * 128:c * CH + (q + 1) * 128, :],
                                xw[:], ("s5.xw",), (f"{dst_tag}.{c}.{q}",))

        stage_l0ffn(x_in, "xin", xA, "xA", "S1")
        P.barrier()
        stage_deltanet(xA, "xA", xB, "xB")
        P.barrier()
        stage_l0ffn(xB, "xB", xA, "xA2", "S3")
        P.barrier()
        stage_bitlin(xA, "xA2", xB, "xB2", "S4")
        P.barrier()
        stage_mamba(xB, "xB2", xA, "xA3")
        P.barrier()
        stage_bitlin(xA, "xA3", out, "out", "S6")
        P.barrier()
        P.finalize(block, sems, esems)
    return nc


_CACHE = {}


def kernel(**inputs):
    if "nc" not in _CACHE:
        _CACHE["nc"] = build_nc()
    nc = _CACHE["nc"]
    x = np.ascontiguousarray(inputs["L0_attn_x"], dtype=np.float32)
    weights = {k: np.ascontiguousarray(v) for k, v in inputs.items()
               if k != "L0_attn_x"}
    in_maps = []
    for b in range(8):
        m = {"x": np.ascontiguousarray(x[b])}
        m.update(weights)
        in_maps.append(m)
    res = run_bass_kernel_spmd(nc, in_maps, core_ids=list(range(8)))
    return np.stack([res.results[b]["out"] for b in range(8)], axis=0)


# version 04
# speedup vs baseline: 1.0590x; 1.0590x over previous
import numpy as np
import contextlib
import concourse.bass as bass
import concourse.mybir as mybir
from concourse.bass_utils import run_bass_kernel_spmd

F32 = mybir.dt.float32
BF16 = mybir.dt.bfloat16
I8 = mybir.dt.int8
I32 = mybir.dt.int32
FP8 = mybir.dt.float8e4
AF = mybir.ActivationFunctionType
ALU = mybir.AluOpType
AX = mybir.AxisListType

T = 4096
DM = 1024
DI = 2048
DS = 16
HF = 4096
CH = 512
NCHUNK = T // CH

KEYS = ("t", "v", "s", "g", "yd", "gd")
ENG_OF = {"t": "tensor", "v": "vector", "s": "scalar", "g": "gpsimd",
          "yd": "sync", "gd": "gpsimd"}
DMA_KEYS = ("yd", "gd")


class Plan:
    def __init__(self, nc):
        self.nc = nc
        self.recs = []

    def op(self, key, fn, reads=(), writes=(), small=False):
        self.recs.append((key, fn, tuple(reads), tuple(writes), small))

    def barrier(self):
        self.recs.append(("BARRIER", None, (), ()))

    def finalize(self, block, sems, esems):
        recs = self.recs
        cnt = {k: 0 for k in KEYS}
        bufs = {}
        waited = {k: {} for k in KEYS}
        needed = set()
        rec_waits = []
        small_of = {}
        for rec in recs:
            key, fn, reads, writes = rec[0], rec[1], rec[2], rec[3]
            small = rec[4] if len(rec) > 4 else False
            if key == "BARRIER":
                bw = {k: cnt[k] for k in KEYS if cnt[k] > 0}
                rec_waits.append(bw)
                for k in KEYS:
                    for pk, ps in bw.items():
                        if pk != k and waited[k].get(pk, 0) < ps:
                            waited[k][pk] = ps
                            needed.add((pk, ps))
                continue
            seq = cnt[key] + 1
            need = {}
            drain = False

            def adddep(pk, ps):
                nonlocal drain
                if pk != key or pk in DMA_KEYS:
                    if pk != key or ps < seq:
                        if need.get(pk, 0) < ps:
                            need[pk] = ps
                elif ps == seq - 1 and small_of.get((pk, ps), False):
                    drain = True

            for b in reads:
                e = bufs.get(b)
                if e and e["w"] is not None:
                    adddep(*e["w"])
            for b in writes:
                e = bufs.get(b)
                if e:
                    if e["w"] is not None:
                        adddep(*e["w"])
                    for (rk, rs) in e["r"]:
                        adddep(rk, rs)
            fw = []
            for pk, ps in need.items():
                if waited[key].get(pk, 0) < ps:
                    fw.append((pk, ps))
                    waited[key][pk] = ps
                    needed.add((pk, ps))
            rec_waits.append((fw, drain))
            cnt[key] = seq
            small_of[(key, seq)] = small
            for b in reads:
                bufs.setdefault(b, {"w": None, "r": []})["r"].append((key, seq))
            for b in writes:
                bufs[b] = {"w": (key, seq), "r": []}
        do_inc = {}
        c2 = {k: 0 for k in KEYS}
        for rec in recs:
            k = rec[0]
            if k == "BARRIER":
                continue
            c2[k] += 1
            do_inc[(k, c2[k])] = (k in DMA_KEYS) or ((k, c2[k]) in needed)
        incs_upto = {}
        for k in KEYS:
            arr = [0] * (cnt[k] + 1)
            acc = 0
            for s in range(1, cnt[k] + 1):
                if do_inc[(k, s)]:
                    acc += 16 if k in DMA_KEYS else 1
                arr[s] = acc
            incs_upto[k] = arr
        echo_ps = {k: set() for k in DMA_KEYS}
        for rw in rec_waits:
            if isinstance(rw, dict):
                continue
            fw, _dr = rw
            for pk, ps in fw:
                if pk in DMA_KEYS:
                    echo_ps[pk].add(ps)
        echo_rank = {}
        for k in DMA_KEYS:
            for r_i, ps in enumerate(sorted(echo_ps[k])):
                echo_rank[(k, ps)] = r_i + 1
        eng_lists = {"tensor": [], "vector": [], "scalar": [], "gpsimd": [], "sync": []}
        c3 = {k: 0 for k in KEYS}
        for i, rec in enumerate(recs):
            key = rec[0]
            if key == "BARRIER":
                for en in eng_lists:
                    eng_lists[en].append(("BWAIT", rec_waits[i], en))
                continue
            c3[key] += 1
            eng_lists[ENG_OF[key]].append((key, c3[key], rec[1], rec_waits[i]))

        def emit(en, eng):
            last_b = None
            for item in eng_lists[en]:
                if item[0] == "BWAIT":
                    bw = item[1]
                    if id(bw) == last_b:
                        continue
                    last_b = id(bw)
                    for pk, ps in bw.items():
                        if ENG_OF[pk] == en and pk not in DMA_KEYS:
                            continue
                        vv = incs_upto[pk][ps]
                        if vv > 0:
                            eng.wait_ge(sems[pk], vv)
                    continue
                key, seq, fn, fwd = item
                fw, drain = fwd
                if drain and key in ("v", "g"):
                    eng.drain()
                for pk, ps in fw:
                    if pk in DMA_KEYS:
                        eng.wait_ge(esems[pk], echo_rank[(pk, ps)])
                        continue
                    if ENG_OF[pk] == en:
                        continue
                    vv = incs_upto[pk][ps]
                    if vv > 0:
                        eng.wait_ge(sems[pk], vv)
                instr = fn(eng)
                if do_inc[(key, seq)]:
                    instr.then_inc(sems[key], 16 if key in DMA_KEYS else 1)
                if key in DMA_KEYS and seq in echo_ps[key]:
                    eng.wait_ge(sems[key], incs_upto[key][seq])
                    eng.sem_inc(esems[key], 1)

        @block.sync
        def _(e):
            emit("sync", e)

        @block.tensor
        def _(e):
            emit("tensor", e)

        @block.vector
        def _(e):
            emit("vector", e)

        @block.scalar
        def _(e):
            emit("scalar", e)

        @block.gpsimd
        def _(e):
            emit("gpsimd", e)


def build_nc():
    nc = bass.Bass("TRN2", target_bir_lowering=False)
    x_in = nc.dram_tensor("x", [T, DM], F32, kind="ExternalInput").ap()
    out = nc.dram_tensor("out", [T, DM], F32, kind="ExternalOutput").ap()
    W = {}
    fspecs = [
        ("L0_attn_Wq", [DM, 128]), ("L0_attn_bq", [128]),
        ("L0_attn_Wk", [DM, 128]), ("L0_attn_bk", [128]),
        ("L0_attn_Wv", [DM, 128]), ("L0_attn_bv", [128]),
        ("L0_attn_Wa", [DM, 1]), ("L0_attn_ba", [1]),
        ("L0_attn_Wb", [DM, 1]), ("L0_attn_bb", [1]),
        ("L0_attn_Wg", [DM, 128]), ("L0_attn_bg", [128]),
        ("L0_attn_Wo", [128, DM]), ("L0_attn_bo", [DM]),
        ("L0_ffn_ln_gamma", [DM]), ("L0_ffn_ln_beta", [DM]),
        ("L0_ffn_fc1_scales", [8, HF]), ("L0_ffn_fc1_bias", [HF]),
        ("L0_ffn_fc2_scales", [32, DM]), ("L0_ffn_fc2_bias", [DM]),
        ("L1_attn_ln_g", [DM]), ("L1_attn_ln_b", [DM]),
        ("L1_attn_W_in", [DM, 2 * DI]),
        ("L1_attn_conv_w", [DI, 4]), ("L1_attn_conv_b", [DI]),
        ("L1_attn_W_xp", [DI, 33]),
        ("L1_attn_W_dt", [1, DI]), ("L1_attn_b_dt", [DI]),
        ("L1_attn_A_log", [DI, DS]), ("L1_attn_D", [DI]),
        ("L1_attn_W_out", [DI, DM]),
        ("L1_ffn_w_gate", [HF, DM]), ("L1_ffn_w_up", [HF, DM]),
        ("L1_ffn_w_down", [DM, HF]),
    ]
    for nm, shp in fspecs:
        W[nm] = nc.dram_tensor(nm, shp, F32, kind="ExternalInput").ap()
    W["L0_ffn_fc1_q"] = nc.dram_tensor("L0_ffn_fc1_q", [DM, HF], I32,
                                       kind="ExternalInput").ap()
    W["L0_ffn_fc2_q"] = nc.dram_tensor("L0_ffn_fc2_q", [HF, DM], I32,
                                       kind="ExternalInput").ap()
    xA = nc.dram_tensor("xA", [T, DM], F32, kind="Internal").ap()
    xB = nc.dram_tensor("xB", [T, DM], F32, kind="Internal").ap()
    xC = nc.dram_tensor("xC", [T, DM], F32, kind="Internal").ap()
    w1d = nc.dram_tensor("w1d", [DM, HF], BF16, kind="Internal").ap()
    w2d = nc.dram_tensor("w2d", [HF, DM], BF16, kind="Internal").ap()
    qgTd = nc.dram_tensor("qgTd", [DM, HF], FP8, kind="Internal").ap()
    quTd = nc.dram_tensor("quTd", [DM, HF], FP8, kind="Internal").ap()
    qdTd = nc.dram_tensor("qdTd", [HF, DM], FP8, kind="Internal").ap()
    zsd = nc.dram_tensor("zsd", [DI, T], BF16, kind="Internal").ap()
    xpjd = nc.dram_tensor("xpjd", [33, 512], F32, kind="Internal").ap()

    es = contextlib.ExitStack()
    with es:
        def sb(name, shape, dt):
            return es.enter_context(nc.sbuf_tensor(name, shape, dt))
        ident_bf = sb("ident_bf", [128, 128], BF16)
        inclT_f = sb("inclT_f", [128, 128], F32)
        maskUneg = sb("maskUneg", [128, 128], F32)
        maskLneg = sb("maskLneg", [128, 128], F32)
        iot = sb("iot", [128, 128], F32)
        ones_row_f = sb("ones_row_f", [1, 128], F32)
        ones_col_f = sb("ones_col_f", [128, 1], F32)
        wqkvg = [sb(f"wqkvg{k}", [128, 512], BF16) for k in range(8)]
        wab = [sb(f"wab{k}", [128, 2], BF16) for k in range(8)]
        wo_b = sb("wo_b", [128, DM], BF16)
        bo_bc = sb("bo_bc", [128, DM], F32)
        bqkvg_bc = sb("bqkvg_bc", [128, 512], F32)
        bab_bc = sb("bab_bc", [128, 2], F32)
        g0_bc = sb("g0_bc", [128, DM], BF16)
        b0_bc = sb("b0_bc", [128, DM], BF16)
        g1_bc = sb("g1_bc", [128, DM], BF16)
        b1_bc = sb("b1_bc", [128, DM], BF16)
        fc1b_fm = sb("fc1b_fm", [128, 32], F32)
        fc2b_bc4 = sb("fc2b_bc4", [128, DM], F32)
        convw = sb("convw", [128, 64], F32)
        convb = sb("convb", [128, 16], F32)
        wdt_c = sb("wdt_c", [128, 16], F32)
        bdt_c = sb("bdt_c", [128, 16], F32)
        A_c = sb("A_c", [128, 16 * DS], F32)
        D_c = sb("D_c", [128, 16], F32)
        wxp = [sb(f"wxp{k}", [128, 33], BF16) for k in range(16)]
        hstate = sb("hstate", [128, 16 * DS], F32)
        halo = sb("halo", [128, 16 * 3], BF16)
        Sst = sb("Sst", [128, 128], F32)
        STst = sb("STst", [128, 128], F32)
        S_bf = sb("S_bf", [128, 128], BF16)
        ST_bf = sb("ST_bf", [128, 128], BF16)
        mug_c = sb("mug_c", [128, 1], F32)
        muu_c = sb("muu_c", [128, 1], F32)
        mud_h = sb("mud_h", [128, 1], F32)
        swg_c = sb("swg_c", [128, 1], F32)
        swu_c = sb("swu_c", [128, 1], F32)
        swd_c = sb("swd_c", [128, 1], F32)
        colw = sb("colw", [128, 96], F32)
        epsc = sb("epsc", [128, 1], F32)
        nhalfc = sb("nhalfc", [128, 1], F32)
        onec = sb("onec", [128, 1], F32)
        rowst = sb("rowst", [1, 512], F32)

        psum = []
        for i in range(8):
            if i in (2, 7):
                psum.append(es.enter_context(
                    nc.psum_tensor(f"pb{i}", [128, 1024], BF16)))
            else:
                psum.append(es.enter_context(
                    nc.psum_tensor(f"pb{i}", [128, 512], F32)))
        BK = [f"bank{i}" for i in range(8)]
        sems = {k: es.enter_context(nc.semaphore(f"sem_{k}")) for k in KEYS}
        esems = {k: es.enter_context(nc.semaphore(f"esem_{k}")) for k in DMA_KEYS}
        block = es.enter_context(nc.Block())
        P = Plan(nc)

        def apsmall(apx):
            try:
                dims = apx.ap
                n = 1
                for st, ct in dims[1:]:
                    n *= ct
                return n <= 64
            except Exception:
                return True

        def ts(outap, inap, s1, s2, op0, op1, rd, wr):
            sm = apsmall(outap)
            if op1 is None:
                P.op("v", lambda e, _o=outap, _i=inap, _a=s1:
                     e.tensor_scalar(_o, _i, _a, None, op0), rd, wr, small=sm)
            else:
                P.op("v", lambda e, _o=outap, _i=inap, _a=s1, _b=s2:
                     e.tensor_scalar(_o, _i, _a, _b, op0, op1), rd, wr, small=sm)

        def tt(outap, a, b, opx, rd, wr):
            P.op("v", lambda e, _o=outap, _a=a, _b=b:
                 e.tensor_tensor(_o, _a, _b, opx), rd, wr, small=apsmall(outap))

        def stt(outap, in0, sc, in1, op0, op1, rd, wr):
            P.op("v", lambda e, _o=outap, _i=in0, _s=sc, _j=in1:
                 e.scalar_tensor_tensor(_o, _i, _s, _j, op0, op1), rd, wr,
                 small=apsmall(outap))

        def act(outap, inap, func, rd, wr, bias=None, scale=None, accum=None):
            kw = {}
            if bias is not None:
                kw["bias"] = bias
            if scale is not None:
                kw["scale"] = scale
            if accum is not None:
                kw["accum_out"] = accum
            P.op("s", lambda e, _o=outap, _i=inap, _k=kw:
                 e.activation(_o, _i, func, **_k), rd, wr)

        def vcopy(outap, inap, rd, wr):
            P.op("v", lambda e, _o=outap, _i=inap: e.tensor_copy(_o, _i), rd, wr,
                 small=apsmall(outap))

        def scopy(outap, inap, rd, wr):
            P.op("s", lambda e, _o=outap, _i=inap: e.copy(_o, _i), rd, wr)

        def vrecip(outap, inap, rd, wr):
            P.op("v", lambda e, _o=outap, _i=inap: e.reciprocal(_o, _i), rd, wr,
                 small=True)

        def mm(outp, lhsT, rhs, start, stop, rd, wr):
            P.op("t", lambda e, _o=outp, _l=lhsT, _r=rhs, _s=start, _p=stop:
                 e.matmul(_o, _l, _r, start=_s, stop=_p), rd, wr)

        def tr(outp, inap, rd, wr):
            P.op("t", lambda e, _o=outp, _i=inap:
                 e.transpose(_o, _i, ident_bf[:]), rd, wr)

        def dma(key, outap, inap, rd, wr):
            P.op(key, lambda e, _o=outap, _i=inap: e.dma_start(_o, _i), rd, wr)

        def vred(outap, inap, op, absv, rd, wr):
            P.op("v", lambda e, _o=outap, _i=inap:
                 e.tensor_reduce(_o, _i, AX.X, op, apply_absolute_value=absv),
                 rd, wr, small=True)

        def memset(ap_, val, wr):
            P.op("v", lambda e, _o=ap_, _v=val: e.memset(_o, _v), (), wr,
                 small=apsmall(ap_))

        def chunk_ap(dram, c):
            return dram[c * CH:(c + 1) * CH, :].rearrange("(a p) d -> p a d", p=128)

        def tile3(tile):
            return tile[:].rearrange("p (a d) -> p a d", a=4)

        def bcast_row(dstap, src_row, n, bank, rdn, wrn, dst_rd_extra=()):
            mm(psum[bank][:, 0:n], ones_row_f[:], src_row, True, True,
               ("ones_row_f",) + tuple(rdn), (BK[bank],))
            vcopy(dstap, psum[bank][:, 0:n], (BK[bank],), wrn)

        P.op("g", lambda e: e.iota(iot[:], [[-1, 128]], base=0,
                                   channel_multiplier=1,
                                   allow_small_or_imprecise_dtypes=True),
             (), ("iot",))
        with contextlib.ExitStack() as stk:
            def psb(name, shape, dt):
                return stk.enter_context(nc.sbuf_tensor(name, shape, dt))
            tmpf = psb("p_tmpf", [128, 128], F32)
            vtmp = psb("p_vtmp", [32, 128], F32)
            vtmpb = psb("p_vtmpb", [32, 128], BF16)

            def load_vec_fm(dst, src1d, nt, wrn):
                dma("yd", vtmp[0:nt, :],
                    src1d.rearrange("(i f) -> i f", f=128), (), ("p_vtmp",))
                vcopy(vtmpb[0:nt, :], vtmp[0:nt, :], ("p_vtmp",), ("p_vtmpb",))
                P.op("t", lambda e, _o=psum[7][:, 0:128], _i=vtmpb[0:nt, :],
                     _id=ident_bf[0:nt, :]: e.transpose(_o, _i, _id),
                     ("p_vtmpb", "ident"), (BK[7],))
                vcopy(dst, psum[7][:, 0:nt], (BK[7],), wrn)
            ts(inclT_f[:], iot[:], 0.0, None, ALU.is_le, None, ("iot",), ("inclT",))
            ts(tmpf[:], iot[:], 0.0, None, ALU.is_lt, None, ("iot",), ("ptmp",))
            ts(maskUneg[:], tmpf[:], -1.0, None, ALU.mult, None, ("ptmp",), ("maskUneg",))
            ts(tmpf[:], iot[:], 0.0, None, ALU.is_gt, None, ("iot",), ("ptmp",))
            ts(maskLneg[:], tmpf[:], -1.0, None, ALU.mult, None, ("ptmp",), ("maskLneg",))
            ts(tmpf[:], iot[:], 0.0, None, ALU.is_equal, None, ("iot",), ("ptmp",))
            vcopy(ident_bf[:], tmpf[:], ("ptmp",), ("ident",))
            memset(ones_row_f[:], 1.0, ("ones_row_f",))
            memset(epsc[:], 1e-5, ("epsc",))
            memset(nhalfc[:], -0.5, ("nhalfc",))
            memset(onec[:], 1.0, ("onec",))
            memset(ones_col_f[:], 1.0, ("ones_col_f",))
            memset(hstate[:], 0.0, ("hstate",))
            memset(halo[:], 0.0, tuple(f"halo{d}" for d in range(16)))
            memset(Sst[:], 0.0, ("Sst",))
            memset(STst[:], 0.0, ("STst",))
            memset(S_bf[:], 0.0, ("S_bf",))
            memset(ST_bf[:], 0.0, ("ST_bf",))
            for k in range(8):
                for j, nm in enumerate(["L0_attn_Wq", "L0_attn_Wk",
                                        "L0_attn_Wv", "L0_attn_Wg"]):
                    dma("gd", wqkvg[k][:, j * 128:(j + 1) * 128],
                        W[nm][k * 128:(k + 1) * 128, :], (), (f"wqkvg{k}",))
                dma("gd", wab[k][:, 0:1], W["L0_attn_Wa"][k * 128:(k + 1) * 128, :],
                    (), (f"wab{k}",))
                dma("gd", wab[k][:, 1:2], W["L0_attn_Wb"][k * 128:(k + 1) * 128, :],
                    (), (f"wab{k}",))
            dma("gd", wo_b[:], W["L0_attn_Wo"], (), ("wo_b",))
            def bc_from(nm, dst, wrn, scale=None, cast=None, width=DM):
                for h in range((width + 511) // 512):
                    n = min(512, width - h * 512)
                    dma("yd", rowst[0:1, 0:n], W[nm][None, h * 512:h * 512 + n],
                        (), ("rowst",))
                    mm(psum[6][:, 0:n], ones_row_f[:], rowst[0:1, 0:n], True, True,
                       ("ones_row_f", "rowst"), (BK[6],))
                    if scale is None:
                        vcopy(dst[:, h * 512:h * 512 + n], psum[6][:, 0:n],
                              (BK[6],), wrn)
                    else:
                        ts(dst[:, h * 512:h * 512 + n], psum[6][:, 0:n], scale, None,
                           ALU.mult, None, (BK[6],), wrn)
            bc_from("L0_attn_bo", bo_bc, ("bo",))
            bc_from("L0_ffn_ln_gamma", g0_bc, ("g0",))
            bc_from("L0_ffn_ln_beta", b0_bc, ("b0",))
            bc_from("L1_attn_ln_g", g1_bc, ("g1",))
            bc_from("L1_attn_ln_b", b1_bc, ("b1",))
            bc_from("L0_ffn_fc2_bias", fc2b_bc4, ("fc2b",), scale=0.25)
            for j, nm in enumerate(["L0_attn_bq", "L0_attn_bk", "L0_attn_bv",
                                    "L0_attn_bg"]):
                dma("yd", rowst[0:1, j * 128:(j + 1) * 128], W[nm][None, :],
                    (), ("rowst",))
            mm(psum[6][:, 0:512], ones_row_f[:], rowst[0:1, 0:512], True, True,
               ("ones_row_f", "rowst"), (BK[6],))
            vcopy(bqkvg_bc[:], psum[6][:, 0:512], (BK[6],), ("bqkvg",))
            dma("yd", rowst[0:1, 0:1], W["L0_attn_ba"][None, :], (), ("rowst",))
            dma("yd", rowst[0:1, 1:2], W["L0_attn_bb"][None, :], (), ("rowst",))
            mm(psum[6][:, 0:2], ones_row_f[:], rowst[0:1, 0:2], True, True,
               ("ones_row_f", "rowst"), (BK[6],))
            vcopy(bab_bc[:], psum[6][:, 0:2], (BK[6],), ("bab",))
            load_vec_fm(fc1b_fm[:], W["L0_ffn_fc1_bias"], 32, ("fc1b",))
            dma("yd", convw[:].rearrange("p (i f) -> p i f", f=4),
                W["L1_attn_conv_w"].rearrange("(i p) f -> p i f", p=128),
                (), ("convw",))
            load_vec_fm(convb[:], W["L1_attn_conv_b"], 16, ("convb",))
            load_vec_fm(wdt_c[:], W["L1_attn_W_dt"][0, :], 16, ("wdt",))
            load_vec_fm(bdt_c[:], W["L1_attn_b_dt"], 16, ("bdt",))
            load_vec_fm(D_c[:], W["L1_attn_D"], 16, ("Dc",))
            dma("yd", A_c[:].rearrange("p (i n) -> p i n", n=DS),
                W["L1_attn_A_log"].rearrange("(i p) n -> p i n", p=128), (), ("Ac",))
            act(A_c[:], A_c[:], AF.Exp, ("Ac",), ("Ac",))
            ts(A_c[:], A_c[:], -1.0, None, ALU.mult, None, ("Ac",), ("Ac",))
            for k in range(16):
                dma("gd", wxp[k][:], W["L1_attn_W_xp"][k * 128:(k + 1) * 128, :],
                    (), (f"wxp{k}",))
        P.barrier()

        with contextlib.ExitStack() as stk:
            def psb(name, shape, dt):
                return stk.enter_context(nc.sbuf_tensor(name, shape, dt))
            qi = psb("dq_qi", [128, HF], I32)
            qf = psb("dq_qf", [128, HF], F32)
            scb = psb("dq_scb", [128, HF], F32)
            wob = psb("dq_wob", [128, HF], BF16)
            for k in range(8):
                dma("yd", qi[:], W["L0_ffn_fc1_q"][k * 128:(k + 1) * 128, :],
                    (), ("dq.qi",))
                vcopy(qf[:], qi[:], ("dq.qi",), ("dq.qf",))
                for h in range(8):
                    dma("yd", rowst[0:1, 0:512],
                        W["L0_ffn_fc1_scales"][k, h * 512:(h + 1) * 512][None, :],
                        (), ("rowst",))
                    mm(psum[6][:, 0:512], ones_row_f[:], rowst[0:1, 0:512],
                       True, True, ("ones_row_f", "rowst"), (BK[6],))
                    vcopy(scb[:, h * 512:(h + 1) * 512], psum[6][:, 0:512],
                          (BK[6],), ("dq.scb",))
                ts(qf[:], qf[:], -128.0, None, ALU.add, None, ("dq.qf",), ("dq.qf",))
                tt(wob[:], qf[:], scb[:], ALU.mult, ("dq.qf", "dq.scb"), ("dq.wob",))
                dma("gd", w1d[k * 128:(k + 1) * 128, :], wob[:], ("dq.wob",),
                    (f"w1d{k}",))
            for k in range(32):
                dma("yd", qi[:, 0:DM], W["L0_ffn_fc2_q"][k * 128:(k + 1) * 128, :],
                    (), ("dq.qi",))
                vcopy(qf[:, 0:DM], qi[:, 0:DM], ("dq.qi",), ("dq.qf",))
                for h in range(2):
                    dma("yd", rowst[0:1, 0:512],
                        W["L0_ffn_fc2_scales"][k, h * 512:(h + 1) * 512][None, :],
                        (), ("rowst",))
                    mm(psum[6][:, 0:512], ones_row_f[:], rowst[0:1, 0:512],
                       True, True, ("ones_row_f", "rowst"), (BK[6],))
                    vcopy(scb[:, h * 512:(h + 1) * 512], psum[6][:, 0:512],
                          (BK[6],), ("dq.scb",))
                ts(qf[:, 0:DM], qf[:, 0:DM], -128.0, None, ALU.add, None,
                   ("dq.qf",), ("dq.qf",))
                tt(wob[:, 0:DM], qf[:, 0:DM], scb[:, 0:DM], ALU.mult,
                   ("dq.qf", "dq.scb"), ("dq.wob",))
                dma("gd", w2d[k * 128:(k + 1) * 128, :], wob[:, 0:DM], ("dq.wob",),
                    (f"w2d{k}",))
        P.barrier()

        with contextlib.ExitStack() as stk:
            def psb(name, shape, dt):
                return stk.enter_context(nc.sbuf_tensor(name, shape, dt))
            wt = psb("bq_wt", [128, HF], F32)
            qf = psb("bq_qf", [128, HF], F32)
            q8 = psb("bq_q8", [128, HF], I8)
            qb = psb("bq_qb", [128, HF], BF16)

            def quant_weight(src, dst, R, C, sw_col, mu_col, mu_scale, tag):
                nt, nb = R // 128, C // 128
                acc = colw[:, 0:1]
                tmpc = colw[:, 1:2]
                for it in range(nt):
                    dma("yd", wt[:, 0:C], src[it * 128:(it + 1) * 128, :],
                        (), (tag + ".wt",))
                    if it == 0:
                        vred(acc, wt[:, 0:C], ALU.add, True, (tag + ".wt",),
                             (tag + ".acc",))
                    else:
                        vred(tmpc, wt[:, 0:C], ALU.add, True, (tag + ".wt",),
                             (tag + ".tc",))
                        tt(acc, acc, tmpc, ALU.add, (tag + ".acc", tag + ".tc"),
                           (tag + ".acc",))
                mm(psum[6][0:1, 0:1], acc, ones_col_f[:], True, True,
                   (tag + ".acc", "ones_col_f"), (BK[6],))
                ts(rowst[0:1, 0:1], psum[6][0:1, 0:1], 1.0 / (R * C), None,
                   ALU.mult, None, (BK[6],), (tag + ".mu",))
                ts(rowst[0:1, 1:2], rowst[0:1, 0:1], 1e-5, None, ALU.max, None,
                   (tag + ".mu",), (tag + ".mu2",))
                vrecip(rowst[0:1, 2:3], rowst[0:1, 1:2], (tag + ".mu2",),
                       (tag + ".sw",))
                mm(psum[6][:, 0:1], ones_row_f[:], rowst[0:1, 2:3], True, True,
                   ("ones_row_f", tag + ".sw"), (BK[6],))
                vcopy(sw_col[:, 0:1], psum[6][:, 0:1], (BK[6],), (tag + ".swc",))
                mm(psum[6][:, 0:1], ones_row_f[:], rowst[0:1, 0:1], True, True,
                   ("ones_row_f", tag + ".mu"), (BK[6],))
                if mu_scale == 1.0:
                    vcopy(mu_col[:, 0:1], psum[6][:, 0:1], (BK[6],), (tag + ".muc",))
                else:
                    ts(mu_col[:, 0:1], psum[6][:, 0:1], mu_scale, None, ALU.mult,
                       None, (BK[6],), (tag + ".muc",))
                stg = [psb(f"{tag}_stg{j}", [128, R], FP8) for j in range(nb)]
                for it in range(nt):
                    dma("yd", wt[:, 0:C], src[it * 128:(it + 1) * 128, :],
                        (), (tag + ".wt",))
                    act(qf[:, 0:C], wt[:, 0:C], AF.Identity, (tag + ".wt", tag + ".swc"),
                        (tag + ".qf",), scale=sw_col[:, 0:1])
                    ts(qf[:, 0:C], qf[:, 0:C], -1.4999999, 1.4999999, ALU.max,
                       ALU.min, (tag + ".qf",), (tag + ".qf",))
                    vcopy(q8[:, 0:C], qf[:, 0:C], (tag + ".qf",), (tag + ".q8",))
                    vcopy(qb[:, 0:C], q8[:, 0:C], (tag + ".q8",), (tag + ".qb",))
                    for jb in range(nb):
                        tr(psum[7][:, 0:128], qb[:, jb * 128:(jb + 1) * 128],
                           (tag + ".qb", "ident"), (BK[7],))
                        scopy(stg[jb][:, it * 128:(it + 1) * 128],
                              psum[7][:, 0:128], (BK[7],), (tag + f".stg{jb}",))
                for jb in range(nb):
                    dma("gd", dst[jb * 128:(jb + 1) * 128, :], stg[jb][:],
                        (tag + f".stg{jb}",), (tag + f".d{jb}",))

            quant_weight(W["L1_ffn_w_gate"], qgTd, HF, DM, swg_c, mug_c, 1.0, "qg")
            quant_weight(W["L1_ffn_w_up"], quTd, HF, DM, swu_c, muu_c, 1.0, "qu")
            quant_weight(W["L1_ffn_w_down"], qdTd, DM, HF, swd_c, mud_h, 0.5, "qd")
        P.barrier()

        def ln_stats(xq, scrap, q, xname, scrname="lnscrap"):
            if hasattr(xq, "ap") and not isinstance(xq, bass.AP):
                xq = xq[:]
            sx = colw[:, 8 * q + 0:8 * q + 1]
            sq = colw[:, 8 * q + 1:8 * q + 2]
            mu = colw[:, 8 * q + 2:8 * q + 3]
            var = colw[:, 8 * q + 3:8 * q + 4]
            vred(sx, xq, ALU.add, False, (xname,), (f"cw{q}.sx",))
            sqb = colw[:, 8 * q + 5:8 * q + 6]
            P.op("s", lambda e, _o=scrap[:, 0:512], _i=xq[:, 0:512], _a=sq:
                 e.activation(_o, _i, AF.Square, accum_out=_a),
                 (xname,), (f"cw{q}.sq", scrname))
            P.op("s", lambda e, _o=scrap[:, 0:512], _i=xq[:, 512:1024], _a=sqb:
                 e.activation(_o, _i, AF.Square, accum_out=_a),
                 (xname,), (f"cw{q}.sqb", scrname))
            tt(sq, sq, sqb, ALU.add, (f"cw{q}.sq", f"cw{q}.sqb"), (f"cw{q}.sq",))
            ts(mu, sx, 1.0 / DM, None, ALU.mult, None, (f"cw{q}.sx",), (f"cw{q}.mu",))
            ts(var, sq, 1.0 / DM, None, ALU.mult, None, (f"cw{q}.sq",), (f"cw{q}.var",))
            ts(sqb, mu, mu, None, ALU.mult, None, (f"cw{q}.mu",), (f"cw{q}.sqb",))
            tt(var, var, sqb, ALU.subtract,
               (f"cw{q}.var", f"cw{q}.sqb"), (f"cw{q}.var",))
            r1 = colw[:, 8 * q + 4:8 * q + 5]
            act(r1, var, AF.Ln, (f"cw{q}.var",), (f"cw{q}.ra",), bias=epsc[:, 0:1])
            act(r1, r1, AF.Exp, (f"cw{q}.ra",), (f"cw{q}.r",), scale=nhalfc[:, 0:1])
            return mu, var, r1

        def inner_ln_cols(mu, var, r1, q):
            var2 = colw[:, 8 * q + 6:8 * q + 7]
            ts(var2, var, r1, None, ALU.mult, None,
               (f"cw{q}.var", f"cw{q}.r"), (f"cw{q}.v2a",))
            ts(var2, var2, r1, None, ALU.mult, None,
               (f"cw{q}.v2a", f"cw{q}.r"), (f"cw{q}.var2",))
            r2 = colw[:, 8 * q + 7:8 * q + 8]
            act(r2, var2, AF.Ln, (f"cw{q}.var2",), (f"cw{q}.r2a",), bias=epsc[:, 0:1])
            act(r2, r2, AF.Exp, (f"cw{q}.r2a",), (f"cw{q}.r2",), scale=nhalfc[:, 0:1])
            sstar = colw[:, 64 + 4 * q + 0:64 + 4 * q + 1]
            bstar = colw[:, 64 + 4 * q + 1:64 + 4 * q + 2]
            ts(sstar, r1, r2, None, ALU.mult, None,
               (f"cw{q}.r", f"cw{q}.r2"), (f"cw{q}.ss",))
            ts(bstar, mu, sstar, -1.0, ALU.mult, ALU.mult,
               (f"cw{q}.mu", f"cw{q}.ss"), (f"cw{q}.bs",))
            return sstar, bstar

        def transpose_tok_to_fm(src, src_name, fm, fm_pref, bank=7):
            for j in range(8):
                for q in range(4):
                    tr(psum[bank][:, q * 128:(q + 1) * 128],
                       src[:, q * 1024 + j * 128:q * 1024 + (j + 1) * 128],
                       (src_name, "ident"), (BK[bank],))
                scopy(fm[j][:], psum[bank][:, 0:512], (BK[bank],),
                      (f"{fm_pref}{j}",))

        def stage_l0ffn(src_dram, src_tag, dst_dram, dst_tag, sn):
            with contextlib.ExitStack() as stk:
                def psb(name, shape, dt):
                    return stk.enter_context(nc.sbuf_tensor(name, shape, dt))
                w1s = [psb(f"{sn}_w1s{k}", [128, HF], BF16) for k in range(8)]
                w2b = psb(f"{sn}_w2b", [128, 8192], BF16)
                hinb = psb(f"{sn}_hinb", [128, 4096], BF16)
                hinfm = [psb(f"{sn}_hf{k}", [128, 512], BF16) for k in range(8)]
                h1fm = [psb(f"{sn}_h1{i}", [128, 512], BF16) for i in range(32)]
                xt = psb(f"{sn}_xt", [128, 4096], F32)
                eq = psb(f"{sn}_eq", [128, DM], F32)
                z2 = psb(f"{sn}_z2", [128, 512], F32)
                xnew = psb(f"{sn}_xnew", [128, 4096], F32)
                for k in range(8):
                    dma("yd", w1s[k][:], w1d[k * 128:(k + 1) * 128, :],
                        (f"w1d{k}",), (f"{sn}.w1{k}",))
                for c in range(NCHUNK):
                    dma("yd", tile3(xt), chunk_ap(src_dram, c),
                        tuple(f"{src_tag}.{c}.{qs}" for qs in range(4)) +
                        (f"{src_tag}.{c}",), (f"{sn}.xt",))
                    for q in range(4):
                        mu, var, r1 = ln_stats(xt[:, q * DM:(q + 1) * DM], eq, q, f"{sn}.xt",
                                               f"{sn}.eq")
                        sstar, bstar = inner_ln_cols(mu, var, r1, q)
                        act(eq[:], xt[:, q * DM:(q + 1) * DM], AF.Identity,
                            (f"{sn}.xt", f"cw{q}.ss", f"cw{q}.bs"), (f"{sn}.eq",),
                            bias=bstar, scale=sstar)
                        tt(eq[:], eq[:], g0_bc[:], ALU.mult,
                           (f"{sn}.eq", "g0"), (f"{sn}.eq",))
                        tt(hinb[:, q * DM:(q + 1) * DM], eq[:], b0_bc[:], ALU.add,
                           (f"{sn}.eq", "b0"), (f"{sn}.hin",))
                    transpose_tok_to_fm(hinb[:], f"{sn}.hin", hinfm, f"{sn}.hf")
                    for i in range(32):
                        bk = i % 2
                        for k in range(8):
                            mm(psum[bk][:], w1s[k][:, i * 128:(i + 1) * 128],
                               hinfm[k][:], k == 0, k == 7,
                               (f"{sn}.w1{k}", f"{sn}.hf{k}"), (BK[bk],))
                        act(h1fm[i][:], psum[bk][:], AF.Silu,
                            (BK[bk], "fc1b"), (f"{sn}.h1{i}",),
                            bias=fc1b_fm[:, i:i + 1])
                    for qp in range(2):
                        for kg in range(8):
                            pb = kg % 2
                            dma("yd",
                                w2b[:, pb * 4096:(pb + 1) * 4096].rearrange(
                                    "p (a d) -> p a d", a=4),
                                w2d[kg * 512:(kg + 1) * 512, :].rearrange(
                                    "(a p) d -> p a d", p=128),
                                tuple(f"w2d{kg * 4 + kk}" for kk in range(4)),
                                (f"{sn}.w2b{pb}",))
                            for kk in range(4):
                                k = kg * 4 + kk
                                for qq in range(2):
                                    q = qp * 2 + qq
                                    for n in range(2):
                                        mm(psum[3 + qq * 2 + n][:, 0:512],
                                           h1fm[k][:, q * 128:(q + 1) * 128],
                                           w2b[:, pb * 4096 + kk * 1024 + n * 512:
                                               pb * 4096 + kk * 1024 + (n + 1) * 512],
                                           k == 0, k == 31,
                                           (f"{sn}.h1{k}", f"{sn}.w2b{pb}"),
                                           (BK[3 + qq * 2 + n],))
                        for qq in range(2):
                            q = qp * 2 + qq
                            r1 = colw[:, 8 * q + 4:8 * q + 5]
                            mu = colw[:, 8 * q + 2:8 * q + 3]
                            s2c = colw[:, 64 + 4 * q + 2:64 + 4 * q + 3]
                            b2c = colw[:, 64 + 4 * q + 3:64 + 4 * q + 4]
                            ts(s2c, r1, 0.5, 1.0, ALU.mult, ALU.add,
                               (f"cw{q}.r",), (f"cw{q}.s2",))
                            ts(b2c, mu, r1, -0.5, ALU.mult, ALU.mult,
                               (f"cw{q}.mu", f"cw{q}.r"), (f"cw{q}.b2",))
                            act(eq[:], xt[:, q * DM:(q + 1) * DM], AF.Identity,
                                (f"{sn}.xt", f"cw{q}.s2", f"cw{q}.b2"),
                                (f"{sn}.eq",), bias=b2c, scale=s2c)
                            for n in range(2):
                                pbk = 3 + qq * 2 + n
                                stt(z2[:], psum[pbk][:, 0:512], 0.25,
                                    fc2b_bc4[:, n * 512:(n + 1) * 512],
                                    ALU.mult, ALU.add,
                                    (BK[pbk], "fc2b"), (f"{sn}.z2",))
                                tt(xnew[:, q * DM + n * 512:q * DM + (n + 1) * 512],
                                   eq[:, n * 512:(n + 1) * 512], z2[:], ALU.add,
                                   (f"{sn}.eq", f"{sn}.z2"), (f"{sn}.xn",))
                    for qs in range(4):
                        dma("gd",
                            dst_dram[c * CH + qs * 128:c * CH + (qs + 1) * 128, :],
                            xnew[:, qs * DM:(qs + 1) * DM],
                            (f"{sn}.xn",), (f"{dst_tag}.{c}.{qs}",))

        def stage_deltanet(src_dram, src_tag, dst_dram, dst_tag):
            sn = "S2"
            with contextlib.ExitStack() as stk:
                def psb(name, shape, dt):
                    return stk.enter_context(nc.sbuf_tensor(name, shape, dt))
                xt = psb("s2_xt", [128, 4096], F32)
                vtb = psb("s2_vtb", [128, 4096], BF16)
                vfm = [psb(f"s2_vf{k}", [128, 512], BF16) for k in range(8)]
                xnew = psb("s2_xnew", [128, 4096], F32)
                qkvg = psb("s2_qkvg", [128, 512], F32)
                ab2 = psb("s2_ab", [128, 2], F32)
                sgab = psb("s2_sgab", [128, 2], F32)
                khat = psb("s2_khat", [128, 128], F32)
                scr = psb("s2_scr", [128, 128], F32)
                t1f = psb("s2_t1f", [128, 128], F32)
                zh = psb("s2_zh", [128, 512], F32)
                Bq = psb("s2_q", [128, 128], BF16)
                Bk = psb("s2_k", [128, 128], BF16)
                Bv = psb("s2_v", [128, 128], BF16)
                Bg = psb("s2_g", [128, 128], F32)
                kT = psb("s2_kT", [128, 128], BF16)
                qT = psb("s2_qT", [128, 128], BF16)
                ksym = psb("s2_ksym", [128, 128], F32)
                P2t = psb("s2_P2", [128, 128], BF16)
                Dm = psb("s2_Dm", [128, 128], F32)
                DmT = psb("s2_DmT", [128, 128], F32)
                zU = psb("s2_zU", [128, 128], F32)
                DmTEi = psb("s2_DmTEi", [128, 128], F32)
                NT = psb("s2_NT", [128, 128], BF16)
                Nn = psb("s2_Nn", [128, 128], BF16)
                Ga = psb("s2_Ga", [128, 128], BF16)
                Gb2 = psb("s2_Gb", [128, 128], BF16)
                Pk1 = psb("s2_Pk1", [128, 128], BF16)
                PkT1 = psb("s2_PkT1", [128, 128], BF16)
                Pk2 = psb("s2_Pk2", [128, 128], BF16)
                PkT2 = psb("s2_PkT2", [128, 128], BF16)
                U0 = psb("s2_U0", [128, 128], BF16)
                MT = psb("s2_MT", [128, 128], BF16)
                Ub = psb("s2_U", [128, 128], BF16)
                UT = psb("s2_UT", [128, 128], BF16)
                WTb = psb("s2_WT", [128, 128], BF16)
                KG = psb("s2_KG", [128, 128], BF16)
                wU = psb("s2_wU", [128, 128], BF16)
                OT = psb("s2_OT", [128, 128], BF16)
                Ob = psb("s2_Ob", [128, 128], BF16)
                scrap = psb("s2_lnscr", [128, DM], F32)
                for c in range(NCHUNK):
                    dma("yd", tile3(xt), chunk_ap(src_dram, c),
                        tuple(f"{src_tag}.{c}.{qs}" for qs in range(4)) +
                        (f"{src_tag}.{c}",), ("s2.xt",))
                    for q in range(4):
                        mu, var, r1 = ln_stats(xt[:, q * DM:(q + 1) * DM], scrap, q, "s2.xt",
                                               "s2.lnscr")
                        b1c = colw[:, 64 + 4 * q:64 + 4 * q + 1]
                        ts(b1c, mu, r1, -1.0, ALU.mult, ALU.mult,
                           (f"cw{q}.mu", f"cw{q}.r"), (f"cw{q}.b1",))
                        act(vtb[:, q * DM:(q + 1) * DM], xt[:, q * DM:(q + 1) * DM],
                            AF.Identity, ("s2.xt", f"cw{q}.r", f"cw{q}.b1"), ("s2.v",),
                            bias=b1c, scale=r1)
                    transpose_tok_to_fm(vtb[:], "s2.v", vfm, "s2.vf")
                    for tg in range(4):
                        for k in range(8):
                            mm(psum[0][:, 0:512], vfm[k][:, tg * 128:(tg + 1) * 128],
                               wqkvg[k][:], k == 0, k == 7,
                               (f"s2.vf{k}", f"wqkvg{k}"), (BK[0],))
                            mm(psum[1][:, 0:2], vfm[k][:, tg * 128:(tg + 1) * 128],
                               wab[k][:], k == 0, k == 7,
                               (f"s2.vf{k}", f"wab{k}"), (BK[1],))
                        tt(qkvg[:], psum[0][:, 0:512], bqkvg_bc[:], ALU.add,
                           (BK[0], "bqkvg"), ("s2.pj",))
                        tt(ab2[:], psum[1][:, 0:2], bab_bc[:], ALU.add,
                           (BK[1], "bab"), ("s2.ab2",))
                        act(sgab[:], ab2[:], AF.Sigmoid, ("s2.ab2",), ("s2.sgab",))
                        act(Bg[:], qkvg[:, 384:512], AF.Sigmoid, ("s2.pj",),
                            ("s2.gate",))
                        zc = colw[:, 80:81]
                        act(zc, sgab[:, 0:1], AF.Ln, ("s2.sgab",), ("s2.zc",))
                        n2 = colw[:, 81:82]
                        act(scr[:], qkvg[:, 128:256], AF.Square,
                            ("s2.pj",), ("s2.n2", "s2.scr"), accum=n2)
                        ts(n2, n2, 1e-24, None, ALU.max, None, ("s2.n2",), ("s2.n2",))
                        rn = colw[:, 82:83]
                        act(rn, n2, AF.Ln, ("s2.n2",), ("s2.rna",))
                        act(rn, rn, AF.Exp, ("s2.rna",), ("s2.rn",), scale=nhalfc[:, 0:1])
                        act(khat[:], qkvg[:, 128:256], AF.Identity,
                            ("s2.pj", "s2.rn"), ("s2.khat",), scale=rn)
                        vcopy(Bk[:], khat[:], ("s2.khat",), ("s2.k",))
                        vcopy(Bq[:], qkvg[:, 0:128], ("s2.pj",), ("s2.q",))
                        vcopy(Bv[:], qkvg[:, 256:384], ("s2.pj",), ("s2.vv",))
                        Lc = colw[:, 83:84]
                        mm(psum[1][:, 4:5], inclT_f[:], zc, True, True,
                           ("inclT", "s2.zc"), (BK[1],))
                        vcopy(Lc, psum[1][:, 4:5], (BK[1],), ("s2.L",))
                        gam = colw[:, 84:85]
                        act(gam, Lc, AF.Exp, ("s2.L",), ("s2.gam",))
                        mm(psum[1][0:1, 6:7], zc, ones_col_f[:], True, True,
                           ("s2.zc", "ones_col_f"), (BK[1],))
                        vcopy(rowst[0:1, 3:4], psum[1][0:1, 6:7], (BK[1],),
                              ("rowst",))
                        mm(psum[1][:, 5:6], ones_row_f[:], rowst[0:1, 3:4],
                           True, True, ("ones_row_f", "rowst"), (BK[1],))
                        LNb = colw[:, 85:86]
                        vcopy(LNb, psum[1][:, 5:6], (BK[1],), ("s2.LN",))
                        gamN = colw[:, 86:87]
                        act(gamN, LNb, AF.Exp, ("s2.LN",), ("s2.gamN",))
                        wv = colw[:, 87:88]
                        tt(wv, LNb, Lc, ALU.subtract, ("s2.LN", "s2.L"), ("s2.wva",))
                        act(wv, wv, AF.Exp, ("s2.wva",), ("s2.wvb",))
                        ts(wv, wv, sgab[:, 1:2], None, ALU.mult, None,
                           ("s2.wvb", "s2.sgab"), ("s2.wv",))
                        tr(psum[2][:, 0:128], Bk[:], ("s2.k", "ident"), (BK[2],))
                        scopy(kT[:], psum[2][:, 0:128], (BK[2],), ("s2.kT",))
                        tr(psum[2][:, 128:256], Bq[:], ("s2.q", "ident"), (BK[2],))
                        scopy(qT[:], psum[2][:, 128:256], (BK[2],), ("s2.qT",))
                        mm(psum[3][:, 0:128], kT[:], kT[:], True, True,
                           ("s2.kT",), (BK[3],))
                        scopy(ksym[:], psum[3][:, 0:128], (BK[3],), ("s2.ksym",))
                        act(P2t[:], ksym[:], AF.Identity, ("s2.ksym", "s2.sgab"),
                            ("s2.P2",), scale=sgab[:, 1:2])
                        ts(zU[:], maskLneg[:], zc, -1.0, ALU.mult, ALU.mult,
                           ("maskLneg", "s2.zc"), ("s2.zU",))
                        mm(psum[3][:, 128:256], zU[:], inclT_f[:], True, True,
                           ("s2.zU", "inclT"), (BK[3],))
                        ts(DmT[:], psum[3][:, 128:256], 0.0, None, ALU.min, None,
                           (BK[3],), ("s2.DmTa",))
                        act(DmT[:], DmT[:], AF.Exp, ("s2.DmTa",), ("s2.DmT",))
                        mm(psum[3][:, 256:384], inclT_f[:], zU[:], True, True,
                           ("s2.zU", "inclT"), (BK[3],))
                        ts(Dm[:], psum[3][:, 256:384], 0.0, None, ALU.min, None,
                           (BK[3],), ("s2.Dma",))
                        act(Dm[:], Dm[:], AF.Exp, ("s2.Dma",), ("s2.Dm",))
                        tt(t1f[:], DmT[:], P2t[:], ALU.mult,
                           ("s2.DmT", "s2.P2"), ("s2.t1",))
                        tt(NT[:], t1f[:], maskUneg[:], ALU.mult,
                           ("s2.t1", "maskUneg"), ("s2.NT",))
                        tr(psum[2][:, 256:384], NT[:], ("s2.NT", "ident"), (BK[2],))
                        scopy(Nn[:], psum[2][:, 256:384], (BK[2],), ("s2.Nn",))
                        tt(Ga[:], NT[:], ident_bf[:], ALU.add,
                           ("s2.NT", "ident"), ("s2.G",))
                        gc, gn = Ga, Gb2
                        pc, pTc = NT, Nn
                        pcn, pTcn = "s2.NT", "s2.Nn"
                        gcn = "s2.G"
                        for it in range(6):
                            pn = Pk1 if it % 2 == 0 else Pk2
                            pTn = PkT1 if it % 2 == 0 else PkT2
                            mm(psum[4][:, 0:128], pTc[:], pc[:], True, True,
                               (pcn, pTcn), (BK[4],))
                            scopy(pn[:], psum[4][:, 0:128], (BK[4],),
                                  (f"s2.P{it % 2}",))
                            mm(psum[4][:, 128:256], pc[:], pTc[:], True, True,
                               (pcn, pTcn), (BK[4],))
                            scopy(pTn[:], psum[4][:, 128:256], (BK[4],),
                                  (f"s2.PT{it % 2}",))
                            mm(psum[5][:, 0:128], pTn[:], gc[:], True, True,
                               (f"s2.PT{it % 2}", gcn), (BK[5],))
                            tt(gn[:], psum[5][:, 0:128], gc[:], ALU.add,
                               (BK[5], gcn), ("s2.G2" if gn is Gb2 else "s2.G",))
                            gcn = "s2.G2" if gn is Gb2 else "s2.G"
                            gc, gn = gn, gc
                            pc, pTc = pn, pTn
                            pcn, pTcn = f"s2.P{it % 2}", f"s2.PT{it % 2}"
                        Gf, Gfn = gc, gcn
                        mm(psum[4][:, 256:384], Gf[:], Bv[:], True, True,
                           (Gfn, "s2.vv"), (BK[4],))
                        scopy(U0[:], psum[4][:, 256:384], (BK[4],), ("s2.U0",))
                        act(KG[:], khat[:], AF.Identity, ("s2.khat", "s2.gam"),
                            ("s2.KG",), scale=gam)
                        mm(psum[4][:, 384:512], KG[:], Gf[:], True, True,
                           ("s2.KG", Gfn), (BK[4],))
                        scopy(MT[:], psum[4][:, 384:512], (BK[4],), ("s2.MT",))
                        mm(psum[5][:, 128:256], MT[:], ST_bf[:], True, True,
                           ("s2.MT", "ST_bf"), (BK[5],))
                        tt(Ub[:], U0[:], psum[5][:, 128:256], ALU.subtract,
                           ("s2.U0", BK[5]), ("s2.U",))
                        tr(psum[2][:, 384:512], Ub[:], ("s2.U", "ident"), (BK[2],))
                        scopy(UT[:], psum[2][:, 384:512], (BK[2],), ("s2.UT",))
                        tt(DmTEi[:], DmT[:], inclT_f[:], ALU.mult,
                           ("s2.DmT", "inclT"), ("s2.DmTEi",))
                        mm(psum[5][:, 256:384], UT[:], qT[:], True, True,
                           ("s2.UT", "s2.qT"), (BK[5],))
                        tt(t1f[:], psum[5][:, 256:384], DmTEi[:], ALU.mult,
                           (BK[5], "s2.DmTEi"), ("s2.wt1",))
                        act(WTb[:], t1f[:], AF.Identity, ("s2.wt1", "s2.sgab"),
                            ("s2.WTb",), scale=sgab[:, 1:2])
                        mm(psum[6][:, 0:128], qT[:], S_bf[:], True, True,
                           ("s2.qT", "S_bf"), (BK[6],))
                        act(t1f[:], psum[6][:, 0:128], AF.Identity,
                            (BK[6], "s2.gam"), ("s2.O1",), scale=gam)
                        mm(psum[6][:, 128:256], WTb[:], Bk[:], True, True,
                           ("s2.WTb", "s2.k"), (BK[6],))
                        tt(t1f[:], t1f[:], psum[6][:, 128:256], ALU.add,
                           ("s2.O1", BK[6]), ("s2.O2",))
                        tt(Ob[:], t1f[:], Bg[:], ALU.mult,
                           ("s2.O2", "s2.gate"), ("s2.O",))
                        act(wU[:], Ub[:], AF.Identity, ("s2.U", "s2.wv"), ("s2.wU",),
                            scale=wv)
                        mm(psum[6][:, 256:384], wU[:], Bk[:], True, True,
                           ("s2.wU", "s2.k"), (BK[6],))
                        stt(Sst[:], Sst[:], gamN, psum[6][:, 256:384],
                            ALU.mult, ALU.add, ("Sst", "s2.gamN", BK[6]), ("Sst",))
                        mm(psum[6][:, 384:512], Bk[:], wU[:], True, True,
                           ("s2.wU", "s2.k"), (BK[6],))
                        stt(STst[:], STst[:], gamN, psum[6][:, 384:512],
                            ALU.mult, ALU.add, ("STst", "s2.gamN", BK[6]), ("STst",))
                        vcopy(S_bf[:], Sst[:], ("Sst",), ("S_bf",))
                        vcopy(ST_bf[:], STst[:], ("STst",), ("ST_bf",))
                        tr(psum[2][:, 0:128], Ob[:], ("s2.O", "ident"), (BK[2],))
                        scopy(OT[:], psum[2][:, 0:128], (BK[2],), ("s2.OT",))
                        for n in range(2):
                            mm(psum[3][:, 384:512] if False else psum[3][:, 0:512],
                               OT[:], wo_b[:, n * 512:(n + 1) * 512], True, True,
                               ("s2.OT", "wo_b"), (BK[3],))
                            tt(zh[:], psum[3][:, 0:512],
                               bo_bc[:, n * 512:(n + 1) * 512], ALU.add,
                               (BK[3], "bo"), ("s2.zh",))
                            tt(xnew[:, tg * DM + n * 512:tg * DM + (n + 1) * 512],
                               zh[:], xt[:, tg * DM + n * 512:tg * DM + (n + 1) * 512],
                               ALU.add, ("s2.zh", "s2.xt"), ("s2.xn",))
                    for qs in range(4):
                        dma("gd",
                            dst_dram[c * CH + qs * 128:c * CH + (qs + 1) * 128, :],
                            xnew[:, qs * DM:(qs + 1) * DM],
                            ("s2.xn",), (f"{dst_tag}.{c}.{qs}",))

        def stage_bitlin(src_dram, src_tag, dst_dram, dst_tag, sn):
            with contextlib.ExitStack() as stk:
                def psb(name, shape, dt):
                    return stk.enter_context(nc.sbuf_tensor(name, shape, dt))
                qgs = [psb(f"{sn}_qg{k}", [128, HF], FP8) for k in range(8)]
                qus = [psb(f"{sn}_qu{k}", [128, HF], FP8) for k in range(8)]
                qds = [psb(f"{sn}_qd{k}", [128, DM], FP8) for k in range(32)]
                xt = psb(f"{sn}_xt", [128, 4096], F32)
                tok = psb(f"{sn}_tok", [128, 4096], BF16)
                ht = psb(f"{sn}_ht", [128, 4096], BF16)
                qxfm = [psb(f"{sn}_qx{k}", [128, 512], BF16) for k in range(8)]
                qhfm = psb(f"{sn}_qhfm", [128, 4096], BF16)
                sc1 = psb(f"{sn}_sc1", [128, 512], BF16)
                sc2 = psb(f"{sn}_sc2", [128, 512], BF16)
                i8s = psb(f"{sn}_i8", [128, DM], I8)
                vq = psb(f"{sn}_vq", [128, DM], F32)
                zd = psb(f"{sn}_zd", [128, 512], F32)
                for k in range(8):
                    dma("yd", qgs[k][:], qgTd[k * 128:(k + 1) * 128, :],
                        (f"qg.d{k}",), (f"{sn}.qg{k}",))
                    dma("yd", qus[k][:], quTd[k * 128:(k + 1) * 128, :],
                        (f"qu.d{k}",), (f"{sn}.qu{k}",))
                for k in range(32):
                    dma("yd", qds[k][:], qdTd[k * 128:(k + 1) * 128, :],
                        (f"qd.d{k}",), (f"{sn}.qd{k}",))
                for c in range(NCHUNK):
                    dma("yd", tile3(xt), chunk_ap(src_dram, c),
                        tuple(f"{src_tag}.{c}.{qs}" for qs in range(4)) +
                        (f"{src_tag}.{c}",), (f"{sn}.xt",))
                    for q in range(4):
                        mu, var, r1 = ln_stats(xt[:, q * DM:(q + 1) * DM], vq, q, f"{sn}.xt",
                                               f"{sn}.vq")
                        b1c = colw[:, 64 + 4 * q:64 + 4 * q + 1]
                        ts(b1c, mu, r1, -1.0, ALU.mult, ALU.mult,
                           (f"cw{q}.mu", f"cw{q}.r"), (f"cw{q}.b1",))
                        act(vq[:], xt[:, q * DM:(q + 1) * DM], AF.Identity,
                            (f"{sn}.xt", f"cw{q}.r", f"cw{q}.b1"), (f"{sn}.vq",),
                            bias=b1c, scale=r1)
                        m = colw[:, 64 + 4 * q + 1:64 + 4 * q + 2]
                        vred(m, vq[:], ALU.max, True, (f"{sn}.vq",), (f"cw{q}.m",))
                        ts(m, m, 1e-5, None, ALU.max, None, (f"cw{q}.m",),
                           (f"cw{q}.m",))
                        im = colw[:, 64 + 4 * q + 2:64 + 4 * q + 3]
                        vrecip(im, m, (f"cw{q}.m",), (f"cw{q}.im",))
                        sxc = colw[:, 64 + 4 * q + 3:64 + 4 * q + 4]
                        ts(sxc, im, 127.0, None, ALU.mult, None, (f"cw{q}.im",),
                           (f"cw{q}.sx",))
                        isx = colw[:, 88 + q:89 + q]
                        ts(isx, m, 1.0 / 127.0, None, ALU.mult, None,
                           (f"cw{q}.m",), (f"cw{q}.isx",))
                        act(vq[:], vq[:], AF.Identity, (f"{sn}.vq", f"cw{q}.sx"),
                            (f"{sn}.vq2",), scale=sxc)
                        vcopy(i8s[:], vq[:], (f"{sn}.vq2",), (f"{sn}.i8",))
                        vcopy(tok[:, q * DM:(q + 1) * DM], i8s[:], (f"{sn}.i8",),
                              (f"{sn}.tok",))
                    transpose_tok_to_fm(tok[:], f"{sn}.tok", qxfm, f"{sn}.qf")
                    for q in range(4):
                        isx = colw[:, 88 + q:89 + q]
                        gsc = colw[:, 92:93]
                        usc = colw[:, 93:94]
                        ts(gsc, isx, mug_c[:, 0:1], None, ALU.mult, None,
                           (f"cw{q}.isx", "qg.muc"), (f"{sn}.gs",))
                        ts(usc, isx, muu_c[:, 0:1], None, ALU.mult, None,
                           (f"cw{q}.isx", "qu.muc"), (f"{sn}.us",))
                        for iq in range(4):
                            for k in range(8):
                                lhs = qxfm[k][:, q * 128:(q + 1) * 128]
                                for n in range(2):
                                    mm(psum[n][:, 0:512], lhs,
                                       qgs[k][:, iq * 1024 + n * 512:
                                               iq * 1024 + (n + 1) * 512],
                                       k == 0, k == 7,
                                       (f"{sn}.qf{k}", f"{sn}.qg{k}"), (BK[n],))
                                for n in range(2):
                                    mm(psum[3 + n][:, 0:512], lhs,
                                       qus[k][:, iq * 1024 + n * 512:
                                               iq * 1024 + (n + 1) * 512],
                                       k == 0, k == 7,
                                       (f"{sn}.qf{k}", f"{sn}.qu{k}"), (BK[3 + n],))
                            for n in range(2):
                                act(sc1[:], psum[n][:, 0:512], AF.Relu,
                                    (BK[n], f"{sn}.gs"), (f"{sn}.rg",), scale=gsc)
                                tt(sc2[:], sc1[:], sc1[:], ALU.mult,
                                   (f"{sn}.rg",), (f"{sn}.rg2",))
                                stt(ht[:, iq * 1024 + n * 512:
                                       iq * 1024 + (n + 1) * 512],
                                    sc2[:], usc, psum[3 + n][:, 0:512],
                                    ALU.mult, ALU.mult,
                                    (f"{sn}.rg2", f"{sn}.us", BK[3 + n]),
                                    (f"{sn}.h",))
                        mh = colw[:, 92:93]
                        vred(mh, ht[:], ALU.max, True, (f"{sn}.h",), (f"{sn}.mh",))
                        ts(mh, mh, 1e-5, None, ALU.max, None, (f"{sn}.mh",),
                           (f"{sn}.mh",))
                        imh = colw[:, 93:94]
                        vrecip(imh, mh, (f"{sn}.mh",), (f"{sn}.imh",))
                        sxh = colw[:, 94:95]
                        ts(sxh, imh, 127.0, None, ALU.mult, None, (f"{sn}.imh",),
                           (f"{sn}.sxh",))
                        isxh = colw[:, 95:96]
                        ts(isxh, mh, 1.0 / 127.0, None, ALU.mult, None,
                           (f"{sn}.mh",), (f"{sn}.isxh",))
                        for hsl in range(8):
                            sl = slice(hsl * 512, (hsl + 1) * 512)
                            act(zd[:], ht[:, sl], AF.Identity,
                                (f"{sn}.h", f"{sn}.sxh"), (f"{sn}.zd",), scale=sxh)
                            vcopy(i8s[:, 0:512], zd[:], (f"{sn}.zd",),
                                  (f"{sn}.i8",))
                            vcopy(tok[:, sl], i8s[:, 0:512], (f"{sn}.i8",),
                                  (f"{sn}.tok",))
                        for jb in range(8):
                            for kk in range(4):
                                k = jb * 4 + kk
                                tr(psum[7][:, kk * 128:(kk + 1) * 128],
                                   tok[:, k * 128:(k + 1) * 128],
                                   (f"{sn}.tok", "ident"), (BK[7],))
                            scopy(qhfm[:, jb * 512:(jb + 1) * 512],
                                  psum[7][:, 0:512], (BK[7],), (f"{sn}.qhf",))
                        for k in range(32):
                            for n in range(2):
                                mm(psum[n][:, 0:512],
                                   qhfm[:, k * 128:(k + 1) * 128],
                                   qds[k][:, n * 512:(n + 1) * 512],
                                   k == 0, k == 31,
                                   (f"{sn}.qhf", f"{sn}.qd{k}"), (BK[n],))
                        rs = colw[:, 92:93]
                        ts(rs, isxh, mud_h[:, 0:1], None, ALU.mult, None,
                           (f"{sn}.isxh", "qd.muc"), (f"{sn}.rs",))
                        for n in range(2):
                            act(zd[:], psum[n][:, 0:512], AF.Identity,
                                (BK[n], f"{sn}.rs"), (f"{sn}.zd",), scale=rs)
                            tt(vq[:, n * 512:(n + 1) * 512],
                               zd[:], xt[:, q * DM + n * 512:q * DM + (n + 1) * 512],
                               ALU.add, (f"{sn}.zd", f"{sn}.xt"), (f"{sn}.vq",))
                        dma("gd", dst_dram[c * CH + q * 128:c * CH + (q + 1) * 128, :],
                            vq[:], (f"{sn}.vq",), (f"{dst_tag}.{c}.{q}",))

        def stage_mamba(src_dram, src_tag, dst_dram, dst_tag):
            sn = "S5"
            with contextlib.ExitStack() as stk:
                def psb(name, shape, dt):
                    return stk.enter_context(nc.sbuf_tensor(name, shape, dt))
                winb = [psb(f"s5_wi{k}", [128, HF], BF16) for k in range(8)]
                wob = psb("s5_wob", [128, 4096], BF16)
                xh_ext = psb("s5_xhe", [128, 515], BF16)
                xnb = psb("s5_xnb", [128, 4096], BF16)
                xnfm = [psb(f"s5_xf{k}", [128, 512], BF16) for k in range(8)]
                dth = [psb(f"s5_dt{d}", [128, 512], BF16) for d in range(8)]
                ugh = [psb(f"s5_ug{d}", [128, 512], BF16) for d in range(8)]
                xcall = [psb(f"s5_xc{d}", [128, 512], BF16) for d in range(16)]
                gtall = [psb(f"s5_gt{d}", [128, 512], BF16) for d in range(16)]
                zstg = psb("s5_zstg", [128, 512], BF16)
                Bbc = [psb("s5_Bbc0", [128, 512], BF16)] * 2
                Cbc = [psb("s5_Cbc0", [128, 512], BF16)] * 2
                wscr = psb("s5_wscr", [128, 512], BF16)
                gscr = psb("s5_gscr", [128, 512], BF16)
                yall = [psb(f"s5_y{d}", [128, 512], F32) for d in range(8)]
                xw = psb("s5_xw", [128, DM], F32)
                dAc = psb("s5_dA", [128, 512], F32)
                cs1 = psb("s5_cs1", [128, 512], F32)
                xpj = psb("s5_xpj", [33, 512], F32)
                cs2 = dAc
                dtrb = cs1
                dtp = dAc
                for k in range(8):
                    dma("gd", winb[k][:], W["L1_attn_W_in"][k * 128:(k + 1) * 128, :],
                        (), (f"s5.wi{k}",))
                for c in range(NCHUNK):
                    for q in range(4):
                        dma("yd", xw[:],
                            src_dram[c * CH + q * 128:c * CH + (q + 1) * 128, :],
                            (f"{src_tag}.{c}.{q}", f"{src_tag}.{c}"), ("s5.xw",))
                        mu, var, r1 = ln_stats(xw, cs1, q, "s5.xw", "s5.cs1")
                        sstar, bstar = inner_ln_cols(mu, var, r1, q)
                        for hh in range(2):
                            hsl = slice(hh * 512, (hh + 1) * 512)
                            act(cs1[:], xw[:, hsl], AF.Identity,
                                ("s5.xw", f"cw{q}.ss", f"cw{q}.bs"), ("s5.cs1",),
                                bias=bstar, scale=sstar)
                            tt(cs1[:], cs1[:], g1_bc[:, hsl], ALU.mult,
                               ("s5.cs1", "g1"), ("s5.cs1",))
                            tt(xnb[:, q * DM + hh * 512:q * DM + (hh + 1) * 512],
                               cs1[:], b1_bc[:, hsl], ALU.add,
                               ("s5.cs1", "b1"), ("s5.xnb",))
                        s3 = colw[:, 64 + 4 * q + 2:64 + 4 * q + 3]
                        b3 = colw[:, 64 + 4 * q + 3:64 + 4 * q + 4]
                        ts(s3, r1, 1.0, None, ALU.add, None, (f"cw{q}.r",),
                           (f"cw{q}.s3",))
                        ts(b3, mu, r1, -1.0, ALU.mult, ALU.mult,
                           (f"cw{q}.mu", f"cw{q}.r"), (f"cw{q}.b3",))
                        act(xw[:], xw[:],
                            AF.Identity, ("s5.xw", f"cw{q}.s3", f"cw{q}.b3"),
                            ("s5.xw",), bias=b3, scale=s3)
                        dma("gd", xC[c * CH + q * 128:c * CH + (q + 1) * 128, :],
                            xw[:], ("s5.xw",), (f"xC.{c}.{q}",))
                    transpose_tok_to_fm(xnb[:], "s5.xnb", xnfm, "s5.xf")
                    for di in range(32):
                        bk = di % 2
                        for k in range(8):
                            mm(psum[bk][:], winb[k][:, di * 128:(di + 1) * 128],
                               xnfm[k][:], k == 0, k == 7,
                               (f"s5.wi{k}", f"s5.xf{k}"), (BK[bk],))
                        if di < 16:
                            vcopy(xh_ext[:, 0:3], halo[:, di * 3:(di + 1) * 3],
                                  (f"halo{di}",), ("s5.xhe",))
                            scopy(xh_ext[:, 3:515], psum[bk][:], (BK[bk],),
                                  ("s5.xhe",))
                            vcopy(halo[:, di * 3:(di + 1) * 3], xh_ext[:, 512:515],
                                  ("s5.xhe",), (f"halo{di}",))
                            act(cs1[:], xh_ext[:, 0:512], AF.Identity,
                                ("s5.xhe", "convw", "convb"), ("s5.cs1",),
                                bias=convb[:, di:di + 1],
                                scale=convw[:, di * 4:di * 4 + 1])
                            stt(cs2[:], xh_ext[:, 1:513],
                                convw[:, di * 4 + 1:di * 4 + 2], cs1[:],
                                ALU.mult, ALU.add,
                                ("s5.xhe", "convw", "s5.cs1"), ("s5.dA",))
                            stt(cs1[:], xh_ext[:, 2:514],
                                convw[:, di * 4 + 2:di * 4 + 3], cs2[:],
                                ALU.mult, ALU.add,
                                ("s5.xhe", "convw", "s5.dA"), ("s5.cs1",))
                            stt(cs2[:], xh_ext[:, 3:515],
                                convw[:, di * 4 + 3:di * 4 + 4], cs1[:],
                                ALU.mult, ALU.add,
                                ("s5.xhe", "convw", "s5.cs1"), ("s5.dA",))
                            act(xcall[di][:], cs2[:], AF.Silu, ("s5.dA",),
                                (f"s5.xc{di}",))
                        else:
                            d2 = di - 16
                            act(zstg[:], psum[bk][:], AF.Silu, (BK[bk],),
                                ("s5.zstg",))
                            dma("gd", zsd[d2 * 128:(d2 + 1) * 128,
                                         c * CH:(c + 1) * CH], zstg[:],
                                ("s5.zstg",), (f"zsd.{c}.{d2}",))
                    for k in range(16):
                        mm(psum[6][0:33, 0:512], wxp[k][:], xcall[k][:],
                           k == 0, k == 15, (f"wxp{k}", f"s5.xc{k}"), (BK[6],))
                    scopy(xpj[:], psum[6][0:33, 0:512], (BK[6],), ("s5.xpj",))
                    dma("gd", xpjd, xpj[:], ("s5.xpj",), ("xpjd",))
                    dma("yd", dtrb[:], xpjd[32:33, :].broadcast_to([128, 512]),
                        ("xpjd",), ("s5.cs1",))
                    for hp in range(2):
                        for d2 in range(8):
                            di = hp * 8 + d2
                            ts(dtp[:], dtrb[:], wdt_c[:, di:di + 1],
                               bdt_c[:, di:di + 1], ALU.mult, ALU.add,
                               ("s5.cs1", "wdt", "bdt"), ("s5.dA",))
                            act(dtp[:], dtp[:], AF.Exp, ("s5.dA",), ("s5.dA",))
                            act(dth[d2][:], dtp[:], AF.Ln, ("s5.dA",),
                                (f"s5.dt{d2}",), bias=onec[:, 0:1])
                            tt(ugh[d2][:], dth[d2][:], xcall[di][:], ALU.mult,
                               (f"s5.dt{d2}", f"s5.xc{di}"), (f"s5.u{d2}",))
                            memset(yall[d2][:], 0.0, (f"s5.y{d2}",))
                        for n in range(DS):
                            pb = n % 2
                            dma("gd", Bbc[pb][:],
                                xpjd[n:n + 1, :].broadcast_to([128, 512]),
                                ("xpjd",), ("s5.Bbc0",))
                            dma("gd", Cbc[pb][:],
                                xpjd[16 + n:17 + n, :].broadcast_to([128, 512]),
                                ("xpjd",), ("s5.Cbc0",))
                            for d2 in range(8):
                                di = hp * 8 + d2
                                act(dAc[:], dth[d2][:], AF.Exp,
                                    (f"s5.dt{d2}", "Ac"), ("s5.dA",),
                                    scale=A_c[:, di * DS + n:di * DS + n + 1])
                                tt(wscr[:], ugh[d2][:], Bbc[pb][:], ALU.mult,
                                   (f"s5.u{d2}", "s5.Bbc0"), ("s5.w",))
                                P.op("v", lambda e, _o=gscr[:], _a=dAc[:],
                                     _b=wscr[:],
                                     _h=hstate[:, di * DS + n:di * DS + n + 1]:
                                     e.tensor_tensor_scan(_o, _a, _b, _h,
                                                          ALU.mult, ALU.add),
                                     ("s5.dA", "s5.w", f"hs{di}"), ("s5.g",))
                                vcopy(hstate[:, di * DS + n:di * DS + n + 1],
                                      gscr[:, 511:512], ("s5.g",), (f"hs{di}",))
                                tt(wscr[:], gscr[:], Cbc[pb][:], ALU.mult,
                                   ("s5.g", "s5.Cbc0"), ("s5.cg",))
                                tt(yall[d2][:], yall[d2][:], wscr[:], ALU.add,
                                   (f"s5.y{d2}", "s5.cg"), (f"s5.y{d2}",))
                        for d2 in range(8):
                            di = hp * 8 + d2
                            stt(yall[d2][:], xcall[di][:], D_c[:, di:di + 1],
                                yall[d2][:], ALU.mult, ALU.add,
                                (f"s5.xc{di}", "Dc", f"s5.y{d2}"), (f"s5.y{d2}",))
                            dma("yd", zstg[:], zsd[di * 128:(di + 1) * 128,
                                                   c * CH:(c + 1) * CH],
                                (f"zsd.{c}.{di}",), ("s5.zld",))
                            tt(gtall[di][:], yall[d2][:], zstg[:], ALU.mult,
                               (f"s5.y{d2}", "s5.zld"), (f"s5.ga{di}",))
                    for qp in range(2):
                        for kg in range(4):
                            dma("gd", wob[:].rearrange("p (a d) -> p a d", a=4),
                                W["L1_attn_W_out"][kg * 512:(kg + 1) * 512, :]
                                .rearrange("(a p) d -> p a d", p=128), (),
                                ("s5.wob",))
                            for kk in range(4):
                                k = kg * 4 + kk
                                for qq in range(2):
                                    q = qp * 2 + qq
                                    for n in range(2):
                                        bkx = (0, 1, 3, 4)[qq * 2 + n]
                                        mm(psum[bkx][:, 0:512],
                                           gtall[k][:, q * 128:(q + 1) * 128],
                                           wob[:, kk * 1024 + n * 512:
                                               kk * 1024 + (n + 1) * 512],
                                           k == 0, k == 15,
                                           (f"s5.ga{k}", "s5.wob"),
                                           (BK[bkx],))
                        for qq in range(2):
                            q = qp * 2 + qq
                            dma("yd", xw[:],
                                xC[c * CH + q * 128:c * CH + (q + 1) * 128, :],
                                (f"xC.{c}.{q}",), ("s5.xw",))
                            for n in range(2):
                                bkx = (0, 1, 3, 4)[qq * 2 + n]
                                tt(xw[:, n * 512:(n + 1) * 512],
                                   psum[bkx][:, 0:512],
                                   xw[:, n * 512:(n + 1) * 512],
                                   ALU.add, (BK[bkx], "s5.xw"), ("s5.xw",))
                            dma("gd",
                                dst_dram[c * CH + q * 128:c * CH + (q + 1) * 128, :],
                                xw[:], ("s5.xw",), (f"{dst_tag}.{c}.{q}",))

        stage_l0ffn(x_in, "xin", xA, "xA", "S1")
        P.barrier()
        stage_deltanet(xA, "xA", xB, "xB")
        P.barrier()
        stage_l0ffn(xB, "xB", xA, "xA2", "S3")
        P.barrier()
        stage_bitlin(xA, "xA2", xB, "xB2", "S4")
        P.barrier()
        stage_mamba(xB, "xB2", xA, "xA3")
        P.barrier()
        stage_bitlin(xA, "xA3", out, "out", "S6")
        P.barrier()
        P.finalize(block, sems, esems)
    return nc


_CACHE = {}


def kernel(**inputs):
    if "nc" not in _CACHE:
        _CACHE["nc"] = build_nc()
    nc = _CACHE["nc"]
    x = np.ascontiguousarray(inputs["L0_attn_x"], dtype=np.float32)
    weights = {k: np.ascontiguousarray(v) for k, v in inputs.items()
               if k != "L0_attn_x"}
    in_maps = []
    for b in range(8):
        m = {"x": np.ascontiguousarray(x[b])}
        m.update(weights)
        in_maps.append(m)
    res = run_bass_kernel_spmd(nc, in_maps, core_ids=list(range(8)))
    return np.stack([res.results[b]["out"] for b in range(8)], axis=0)
